# version 16
# baseline (speedup 1.0000x reference)
import numpy as np
import ml_dtypes
from contextlib import ExitStack

import concourse.bass as bass
import concourse.mybir as mybir
from concourse.bass_utils import run_bass_kernel_spmd

F32 = mybir.dt.float32
BF16 = mybir.dt.bfloat16
AF = mybir.ActivationFunctionType
ALU = mybir.AluOpType
AX = mybir.AxisListType

NSEQ = 4
S = 2048
D = 1024
H = 16
HD = 64
NT = S // 128
NQG = 4
NE = 18
FG = 1024
NFG = S // FG
JT_G = FG // 128
INV_D = 1.0 / 1024.0
EPS = np.float32(1e-5)
NEG1 = -1.0
NEG1_S = np.float32(-1.0)
NEGH = np.float32(-0.5)
QSCALE = np.float32(0.125)
EXPB = np.float32(-30.0)
BIG = 1e9
NBIG = -1e9

CHUNKS = [(0, 1, 2, 3), (4, 5, 6, 7), (8, 9, 10, 11), (12, 13, 14, 15), (16, 17)]

_PHASES = "ABF"
_NSEQ_RUN = NSEQ
_B_LIMIT = "full"


class Sem:
    def __init__(self, stack, nc, name):
        self.h = stack.enter_context(nc.semaphore(name))
        self.v = 0


class Emitter:

    def __init__(self, nc):
        self.nc = nc
        self.ops = []
        self.waited = {}

    def op(self, eng, method, *args, incs=(), **kwargs):
        self.ops.append((eng, method, args, kwargs, list(incs)))
        for sem, k in incs:
            sem.v += k

    def wait(self, eng, sem, val=None):
        v = sem.v if val is None else val
        key = (eng, sem.h.num)
        if v > 0 and v > self.waited.get(key, 0):
            self.waited[key] = v
            self.ops.append((eng, "wait_ge", (sem.h, v), {}, []))

    def waits(self, eng, deps):
        for sem, tok in deps:
            self.wait(eng, sem, tok)

    def flush(self):
        ops = self.ops
        self.ops = []
        if not ops:
            return
        with self.nc.Block() as blk:
            for name in ("sync", "gpsimd", "tensor", "vector", "scalar"):
                mine = [o for o in ops if o[0] == name]
                if not mine:
                    continue

                def body(eng, mine=mine):
                    for (_, method, args, kwargs, incs) in mine:
                        if method == "wait_ge":
                            eng.wait_ge(*args)
                            continue
                        bi = getattr(eng, method)(*args, **kwargs)
                        for sem, k in incs:
                            bi.then_inc(sem.h, k)

                getattr(blk, name)(body)


def _ln_stats(em, T, stw, src_ap, sqd, col0, deps_v, deps_s):
    vV, sS = T["vV"], T["sS"]
    c = lambda i: stw[:, col0 + i:col0 + i + 1]
    em.waits("vector", deps_v)
    em.waits("scalar", deps_s)
    em.op("vector", "reduce_sum", c(0), src_ap, axis=AX.X, incs=[(vV, 1)])
    em.wait("scalar", vV, vV.v)
    em.op("scalar", "activation", sqd[:], src_ap, AF.Square,
          accum_out=c(1), incs=[(sS, 1)])
    tok_sq = sS.v
    em.wait("vector", vV)
    em.op("vector", "tensor_scalar", c(2), c(0), INV_D, None, ALU.mult, incs=[(vV, 1)])
    em.wait("vector", sS, tok_sq)
    em.op("vector", "tensor_scalar", c(3), c(1), INV_D, None, ALU.mult, incs=[(vV, 1)])
    em.wait("vector", vV)
    em.op("vector", "tensor_scalar", c(4), c(2), c(2), None, ALU.mult, incs=[(vV, 1)])
    em.wait("vector", vV)
    em.op("vector", "tensor_tensor", c(5), c(3), c(4), ALU.subtract, incs=[(vV, 1)])
    em.wait("scalar", vV, vV.v)
    em.op("scalar", "activation", c(6), c(5), AF.Ln, bias=EPS, incs=[(sS, 1)])
    em.wait("scalar", sS)
    em.op("scalar", "activation", c(5), c(6), AF.Exp, scale=NEGH, incs=[(sS, 1)])
    tok_r = sS.v
    em.wait("vector", sS, tok_r)
    em.op("vector", "tensor_scalar", c(6), c(2), c(5), NEG1, ALU.mult, ALU.mult,
          incs=[(vV, 1)])
    tok_nm = vV.v
    return c(5), c(6), tok_nm, tok_r


def build_program():
    nc = bass.Bass("TRN2", target_bir_lowering=False)

    xin = nc.dram_tensor("xin", [NSEQ, S, D], F32, kind="ExternalInput").ap()
    dr = {}
    for nm, shp in [("wq", [128, 8192]), ("wk", [128, 8192]), ("wv", [128, 8192]),
                    ("wo", [128, 8192]), ("gwt", [128, 128]), ("gb16", [128, 16]),
                    ("bia1", [128, 72]), ("bia3", [128, 72]), ("be2", [16, 1024]),
                    ("bs2", [1, 1024]), ("we13", [NE, 128, 8192]),
                    ("we2", [NE, 128, 4096]), ("cident", [128, 128]),
                    ("cmask", [128, 2048]), ("csel", [16, 2048])]:
        dr[nm] = nc.dram_tensor(nm, shp, F32, kind="ExternalInput").ap()
    xout = nc.dram_tensor("xout", [NSEQ, S, D], F32, kind="ExternalOutput").ap()
    x2scr = nc.dram_tensor("x2scr", [NSEQ, S, D], F32, kind="Internal").ap()
    dr["wbf13"] = nc.dram_tensor("wbf13", [NE, 128, 8192], BF16, kind="Internal").ap()
    dr["wbf2"] = nc.dram_tensor("wbf2", [NE, 128, 4096], BF16, kind="Internal").ap()

    st = ExitStack()
    with st:
        em = Emitter(nc)
        T = {"nc": nc, "em": em, "xin": xin, "xout": xout, "x2scr": x2scr, "dr": dr}
        for nm in ("dWB", "dWBs", "dWSt", "dWSt2", "dPro", "dWa0", "dWa1", "dWo", "dXa0", "dXa1", "dXb",
                   "dF13_0", "dF13_1", "dF2_0", "dF2_1", "dF2_2", "dF2_3",
                   "dFc", "dS0", "dS1", "dXf0", "dXf1", "dO0", "dO1",
                   "tPE", "vV", "sS"):
            T[nm] = Sem(st, nc, nm)
        dPro = T["dPro"]

        sb = lambda name, shp, dt=BF16: st.enter_context(nc.sbuf_tensor(name, shp, dt))
        T["ident"] = sb("ident", [128, 128])
        T["masks"] = sb("masks", [128, 2048])
        T["gwt"] = sb("gwtS", [128, 128])
        T["gb16"] = sb("gb16S", [128, 16], F32)
        T["bia1"] = sb("bia1S", [128, 72], F32)
        T["bia3"] = sb("bia3S", [128, 72], F32)
        T["onec"] = sb("onec", [1, 128])
        T["fmx"] = sb("fmx", [128, 8 * 2048])
        T["scr"] = sb("scr", [128, 1024], F32)
        T["ones1f"] = sb("ones1f", [1, 64], F32)

        for dst, srcnm in [(T["ident"], "cident"), (T["masks"], "cmask"),
                           (T["gwt"], "gwt"),
                           (T["gb16"], "gb16"), (T["bia1"], "bia1"),
                           (T["bia3"], "bia3")]:
            em.op("gpsimd", "dma_start", dst[:], dr[srcnm][:], incs=[(dPro, 16)])
        em.op("vector", "memset", T["onec"][:], 1.0)
        em.op("vector", "memset", T["ones1f"][:], 1.0, incs=[(T["vV"], 1)])
        T["tok_prologue"] = dPro.v
        em.flush()

        for s in range(_NSEQ_RUN):
            _emit_seq(nc, st, em, s, T)

        em.wait("sync", T["dO0"])
        em.wait("sync", T["dO1"])
        em.flush()

    nc.finalize()
    return nc


def _emit_seq(nc, st, em, s, T):
    xin, xout, x2scr, dr = T["xin"], T["xout"], T["x2scr"], T["dr"]
    fmx, ident = T["fmx"], T["ident"]
    dPro = T["dPro"]
    dWa = [T["dWa0"], T["dWa1"]]
    dXa = [T["dXa0"], T["dXa1"]]
    dXb = T["dXb"]
    dS = [T["dS0"], T["dS1"]]
    tPE, vV, sS = T["tPE"], T["vV"], T["sS"]
    fm = lambda c, t0, n: fmx[:, c * 2048 + t0: c * 2048 + t0 + n]

    if "A" not in _PHASES:
        return
    for _e in ("sync", "gpsimd", "tensor", "vector", "scalar"):
        em.wait(_e, T["dO0"])
        em.wait(_e, T["dO1"])
        em.wait(_e, T["dXf0"])
        em.wait(_e, T["dXf1"])
    with ExitStack() as qst:
        stg = (qst.enter_context(nc.sbuf_tensor("stg0", [128, 4096], BF16))
               if s == 0 else None)
        q_fm = qst.enter_context(nc.sbuf_tensor(f"q_fm{s}", [128, 8 * 2048], BF16))
        k_fm = qst.enter_context(nc.sbuf_tensor(f"k_fm{s}", [128, 8 * 2048], BF16))
        v_tm = qst.enter_context(nc.sbuf_tensor(f"v_tm{s}", [128, 16 * 1040], BF16))

        with ExitStack() as ast:
            xt = [ast.enter_context(nc.sbuf_tensor(f"xtA{s}_{i}", [128, 1024], F32))
                  for i in range(2)]
            ln1t = [ast.enter_context(nc.sbuf_tensor(f"ln1t{s}_{i}", [128, 1024], BF16))
                    for i in range(2)]
            stw = [ast.enter_context(nc.sbuf_tensor(f"stA{s}_{i}", [128, 16], F32))
                   for i in range(2)]
            wbufA = [ast.enter_context(nc.sbuf_tensor(f"wA{s}_{i}", [128, 8192], BF16))
                     for i in range(2)]
            ps = [ast.enter_context(nc.psum_tensor(f"psA{s}_{i}", [128, 512], F32))
                  for i in range(2)]
            tp = [ast.enter_context(nc.psum_tensor(f"tpA{s}_{i}", [128, 1024], BF16))
                  for i in range(2)]

            em.op("gpsimd", "dma_start", wbufA[0][:], dr["wk"][:], incs=[(dWa[0], 16)])
            tok_wk = (dWa[0], dWa[0].v)
            em.op("gpsimd", "dma_start", wbufA[1][:], dr["wv"][:], incs=[(dWa[1], 16)])
            tok_wv = (dWa[1], dWa[1].v)

            xt_vrd = [0, 0]
            xt_srd = [0, 0]
            tp_rd = [0, 0]
            ln1t_rd = [0, 0]
            st_rd = [0, 0]
            ln1_tok = [0] * NT

            ps_rd = [0, 0]
            ridx = [0]

            def emit_tile_ln(qt):
                sl = qt % 2
                em.wait("sync", vV, xt_vrd[sl])
                em.wait("sync", sS, xt_srd[sl])
                em.op("sync", "dma_start", xt[sl][:],
                      xin[s, qt * 128:(qt + 1) * 128, :], incs=[(dXa[sl], 16)])
                tok_x = dXa[sl].v
                em.wait("vector", sS, st_rd[sl])
                r_ap, nm_ap, tok_nm, tok_r = _ln_stats(
                    em, T, stw[sl], xt[sl][:], T["scr"], 0,
                    [(dXa[sl], tok_x)], [(dXa[sl], tok_x)])
                em.wait("scalar", tPE, ln1t_rd[sl])
                em.wait("scalar", vV, tok_nm)
                em.op("scalar", "activation", ln1t[sl][:], xt[sl][:], AF.Identity,
                      bias=nm_ap, scale=r_ap, incs=[(sS, 1)])
                st_rd[sl] = sS.v
                xt_srd[sl] = sS.v
                xt_vrd[sl] = tok_nm
                em.wait("tensor", sS, sS.v)
                em.wait("tensor", vV, tp_rd[sl])
                if qt == 0 and s == 0:
                    em.wait("tensor", dPro, T["tok_prologue"])
                for c_ in range(8):
                    em.op("tensor", "transpose", tp[sl][:, c_ * 128:(c_ + 1) * 128],
                          ln1t[sl][:, c_ * 128:(c_ + 1) * 128], ident[:],
                          incs=([(tPE, 1)] if c_ == 7 else []))
                ln1t_rd[sl] = tPE.v
                em.wait("vector", tPE, tPE.v)
                em.op("vector", "tensor_copy",
                      fmx[:].rearrange("p (c t) -> p c t", c=8)[:, :, qt * 128:(qt + 1) * 128],
                      tp[sl][:].rearrange("p (c u) -> p c u", c=8),
                      incs=[(vV, 1)])
                tp_rd[sl] = vV.v
                ln1_tok[qt] = vV.v

            def proj_tch(dst, wsb, tok_w, scale, tch, fo_list):
                em.wait("tensor", tok_w[0], tok_w[1])
                em.wait("tensor", vV, ln1_tok[tch * 4 + 3])
                for fo in fo_list:
                    sl = ridx[0] % 2
                    ridx[0] += 1
                    em.wait("tensor", sS, ps_rd[sl])
                    for kc in range(8):
                        em.op("tensor", "matmul", ps[sl][:],
                              wsb[:, kc * 1024 + fo * 128: kc * 1024 + fo * 128 + 128],
                              fm(kc, tch * 512, 512),
                              start=(kc == 0), stop=(kc == 7),
                              incs=([(tPE, 1)] if kc == 7 else []))
                    em.wait("scalar", tPE, tPE.v)
                    em.op("scalar", "activation",
                          dst[:, fo * 2048 + tch * 512: fo * 2048 + (tch + 1) * 512],
                          ps[sl][:], AF.Identity, scale=scale, incs=[(sS, 1)])
                    ps_rd[sl] = sS.v
                return tPE.v

            def proj_fm(dst, wsb, tok_w, scale):
                for fo in range(8):
                    for tch in range(4):
                        proj_tch(dst, wsb, tok_w, scale, tch, [fo])
                return tPE.v

            for tch in range(4):
                for qt in range(tch * 4, tch * 4 + 4):
                    emit_tile_ln(qt)
                proj_tch(k_fm, wbufA[0], tok_wk, np.float32(1.0), tch,
                         list(range(8)))
            tok_kmm = tPE.v
            em.wait("gpsimd", tPE, tok_kmm)
            em.op("gpsimd", "dma_start", wbufA[0][:], dr["wq"][:], incs=[(dWa[0], 16)])
            tok_wq = (dWa[0], dWa[0].v)
            if s == 0:
                dWBl = [T["dWB"], T["dWBs"]]
                dWStl = [T["dWSt"], T["dWSt2"]]
                st_tok = [0, 0]
                jobs = []
                for e in range(NE):
                    for off in (0, 2048, 4096, 6144):
                        jobs.append((dr["we13"], dr["wbf13"], e, off, 2048))
                    for off in (0, 2048):
                        jobs.append((dr["we2"], dr["wbf2"], e, off, 2048))
                for ji, (srcd, dstd, e, off, ncol) in enumerate(jobs):
                    sl = ji % 2
                    view = stg[:, sl * 2048: sl * 2048 + ncol]
                    em.wait("gpsimd", dWStl[sl], st_tok[sl])
                    em.op("gpsimd", "dma_start", view, srcd[e, :, off:off + ncol],
                          incs=[(dWBl[sl], 16)])
                    em.wait("gpsimd", dWBl[sl], dWBl[sl].v)
                    em.op("gpsimd", "dma_start", dstd[e, :, off:off + ncol], view,
                          incs=[(dWStl[sl], 16)])
                    st_tok[sl] = dWStl[sl].v
                T["tok_wbf"] = (T["dWSt"].v, T["dWSt2"].v)
            em.op("vector", "memset",
                  v_tm[:].rearrange("p (a x) -> p a x", x=65)[:, :, 64:65], 1.0,
                  incs=[(vV, 1)])
            tok_vones = vV.v
            em.wait("tensor", tok_wv[0], tok_wv[1])
            em.wait("tensor", vV, tok_vones)
            for qt in range(NT):
                em.wait("tensor", vV, ln1_tok[qt])
                for fo in range(2):
                    sl = ridx[0] % 2
                    ridx[0] += 1
                    em.wait("tensor", sS, ps_rd[sl])
                    for kc in range(8):
                        em.op("tensor", "matmul", ps[sl][:],
                              fm(kc, qt * 128, 128),
                              wbufA[1][:, kc * 1024 + fo * 512: kc * 1024 + (fo + 1) * 512],
                              start=(kc == 0), stop=(kc == 7),
                              incs=([(tPE, 1)] if kc == 7 else []))
                    em.wait("scalar", tPE, tPE.v)
                    dstv = v_tm[:, qt * 1040 + fo * 520: qt * 1040 + (fo + 1) * 520]
                    em.op("scalar", "activation",
                          dstv.rearrange("p (h x) -> p h x", x=65)[:, :, 0:64],
                          ps[sl][:].rearrange("p (h x) -> p h x", x=64),
                          AF.Identity, incs=[(sS, 1)])
                    ps_rd[sl] = sS.v
            proj_fm(q_fm, wbufA[0], tok_wq, QSCALE)
            em.flush()

        if "B" not in _PHASES:
            em.flush()
            return
        for _e in ("sync", "gpsimd", "tensor", "vector", "scalar"):
            em.wait(_e, T["dXa0"])
            em.wait(_e, T["dXa1"])
            em.wait(_e, T["dWa0"])
            em.wait(_e, T["dWa1"])
        with ExitStack() as bst:
            wo_sb = bst.enter_context(nc.sbuf_tensor(f"wo_sb{s}", [128, 8192], BF16))
            pt_sb = [bst.enter_context(nc.sbuf_tensor(f"ptsb{s}_{i}", [128, 1024], BF16))
                     for i in range(2)]
            o_sb = [bst.enter_context(nc.sbuf_tensor(f"osb{s}_{i}", [128, 512], F32))
                    for i in range(2)]
            r_row = [bst.enter_context(nc.sbuf_tensor(f"rrow{s}_{i}", [1, 512], BF16))
                     for i in range(2)]
            lrow32 = bst.enter_context(nc.sbuf_tensor(f"lrow{s}", [1, 512], F32))
            ofm = bst.enter_context(nc.sbuf_tensor(f"ofm{s}", [128, 8 * 512], BF16))
            atsb = [bst.enter_context(nc.sbuf_tensor(f"atsb{s}_{i}", [128, 1024], F32))
                    for i in range(2)]
            xtB = [bst.enter_context(nc.sbuf_tensor(f"xtB{s}", [128, 1024], F32))]
            x2t = [bst.enter_context(nc.sbuf_tensor(f"x2t{s}_{i}", [128, 1024], F32))
                   for i in range(2)]
            ln3t = [bst.enter_context(nc.sbuf_tensor(f"ln3t{s}_{i}", [128, 1024], BF16))
                    for i in range(2)]
            stwB = [bst.enter_context(nc.sbuf_tensor(f"stB{s}_{i}", [128, 16], F32))
                    for i in range(2)]
            s_ps = [bst.enter_context(nc.psum_tensor(f"sps{s}_{i}", [128, 1024], F32))
                    for i in range(2)]
            ptps = [bst.enter_context(nc.psum_tensor(f"ptps{s}_0", [128, 1024], BF16))]
            o_ps = [bst.enter_context(nc.psum_tensor(f"ops{s}_{i}", [128, 512], F32))
                    for i in range(2)]
            at_ps = bst.enter_context(nc.psum_tensor(f"atps{s}", [128, 512], F32))
            rb_ps = at_ps

            em.op("gpsimd", "dma_start", wo_sb[:], dr["wo"][:], incs=[(T["dWo"], 16)])
            tok_wo = T["dWo"].v

            sps_rd = [0, 0]
            pt_rd = [0, 0]
            ptps_rd = [[]]
            ops_rd = [[], []]
            osb_rd = [0, 0]
            rrow_rd = [0, 0]
            rb_rd = 0
            atps_rd = 0
            xtB_tok = [0, 0]
            xtB_rd = [0, 0]
            x2t_rd = [[], []]
            stB_rd = [0, 0]
            ln3t_rd = [0, 0]
            atsb_rd = [0, 0]
            pending_ln3 = []
            mask_wait = True

            rr = 0
            orot = 0
            def emit_pair_mms(qg, c):
                nonlocal rr, mask_wait
                ch = c
                nkt = 4 * (qg + 1)
                exp_tok = [0] * nkt

                def emit_st(kt):
                    nonlocal rr, mask_wait
                    sl = rr % 2
                    rr += 1
                    em.wait("tensor", sS, sps_rd[sl])
                    for hh in range(2):
                        pb = 64 * hh
                        em.op("tensor", "matmul",
                              s_ps[sl][:, hh * 512:(hh + 1) * 512],
                              k_fm[pb:pb + 64, ch * 2048 + kt * 128: ch * 2048 + (kt + 1) * 128],
                              q_fm[pb:pb + 64, ch * 2048 + qg * 512: ch * 2048 + (qg + 1) * 512],
                              start=True, stop=True, incs=[(tPE, 1)])
                    tok_st = (tPE, tPE.v)
                    r = kt - qg * 4
                    if r >= 0:
                        em.wait("vector", tPE, tok_st[1])
                        if mask_wait:
                            em.wait("vector", dPro, T["tok_prologue"])
                            mask_wait = False
                        for hh in range(2):
                            em.op("vector", "tensor_tensor",
                                  s_ps[sl][:, hh * 512:(hh + 1) * 512],
                                  s_ps[sl][:, hh * 512:(hh + 1) * 512],
                                  T["masks"][:, r * 512:(r + 1) * 512],
                                  ALU.add, incs=[(vV, 1)])
                        tok_st = (vV, vV.v)
                    em.wait("scalar", tok_st[0], tok_st[1])
                    em.wait("scalar", tPE, pt_rd[sl])
                    em.op("scalar", "activation", pt_sb[sl][:], s_ps[sl][:],
                          AF.Exp, bias=EXPB, incs=[(sS, 1)])
                    sps_rd[sl] = sS.v
                    exp_tok[kt] = sS.v
                    return sl

                slots = [emit_st(kk) for kk in range(min(2, nkt))]
                for kt in range(nkt):
                    sl = slots[kt]
                    em.wait("tensor", sS, exp_tok[kt])
                    if kt == 0:
                        em.waits("tensor", ops_rd[0])
                        em.waits("tensor", ops_rd[1])
                        ops_rd[0] = []
                        ops_rd[1] = []
                    for hh in range(2):
                        h = 2 * c + hh
                        em.op("tensor", "matmul", o_ps[hh][0:65, :],
                              v_tm[:, kt * 1040 + h * 65: kt * 1040 + (h + 1) * 65],
                              pt_sb[sl][:, hh * 512:(hh + 1) * 512],
                              start=(kt == 0), stop=(kt == nkt - 1),
                              incs=([(tPE, 1)] if hh == 1 else []))
                    pt_rd[sl] = tPE.v
                    if kt + 2 < nkt:
                        slots.append(emit_st(kt + 2))
                return (c, tPE.v)

            def emit_fin_a(fin):
                c, tok_pv = fin
                toks = []
                for hh in range(2):
                    em.wait("scalar", tPE, tok_pv)
                    em.wait("scalar", vV, osb_rd[hh])
                    em.op("scalar", "activation", o_sb[hh][0:64, 0:512],
                          o_ps[hh][0:64, :], AF.Identity, incs=[(sS, 1)])
                    tok_osb = sS.v
                    em.wait("scalar", sS)
                    em.op("scalar", "activation", lrow32[0:1, :],
                          o_ps[hh][64:65, 0:512], AF.Ln, incs=[(sS, 1)])
                    em.wait("scalar", sS)
                    em.wait("scalar", tPE, rrow_rd[hh])
                    em.op("scalar", "activation", r_row[hh][0:1, :],
                          lrow32[0:1, :], AF.Exp, scale=NEG1_S, incs=[(sS, 1)])
                    tok_rr = sS.v
                    ops_rd[hh] = [(sS, tok_rr)]
                    toks.append((tok_osb, tok_rr))
                return (c, tok_pv, toks)

            def emit_fin_b(fin):
                nonlocal rb_rd
                c, tok_pv, toks = fin
                for hh in range(2):
                    tok_osb, tok_rr = toks[hh]
                    pb = 64 * hh
                    em.wait("tensor", sS, tok_rr)
                    em.wait("tensor", vV, rb_rd)
                    em.wait("tensor", sS, atps_rd)
                    em.op("tensor", "matmul", rb_ps[0:64, :],
                          T["onec"][0:1, 0:64], r_row[hh][0:1, :],
                          start=True, stop=True, incs=[(tPE, 1)])
                    tok_rb = tPE.v
                    rrow_rd[hh] = tok_rb
                    em.wait("vector", tPE, tok_rb)
                    em.wait("vector", sS, tok_osb)
                    em.op("vector", "tensor_tensor",
                          ofm[pb:pb + 64, c * 512:(c + 1) * 512],
                          o_sb[hh][0:64, 0:512], rb_ps[0:64, :], ALU.mult,
                          incs=[(vV, 1)])
                    rb_rd = vV.v
                    osb_rd[hh] = vV.v

            pending_fin = None
            for qg in range(NQG):
                for c in range(H // 2):
                    if pending_fin is not None:
                        pending_fin = emit_fin_a(pending_fin)
                    fin = emit_pair_mms(qg, c)
                    if pending_fin is not None:
                        emit_fin_b(pending_fin)
                    pending_fin = fin
                pending_fin = emit_fin_a(pending_fin)
                emit_fin_b(pending_fin)
                pending_fin = None
                tok_ofm = vV.v
                def emit_ln3_tp(job):
                    qt_, slx_, tok_apply = job
                    em.wait("tensor", sS, tok_apply)
                    em.waits("tensor", ptps_rd[0])
                    ptps_rd[0] = []
                    for c_ in range(8):
                        em.op("tensor", "transpose",
                              ptps[0][:, c_ * 128:(c_ + 1) * 128],
                              ln3t[slx_][:, c_ * 128:(c_ + 1) * 128], ident[:],
                              incs=([(tPE, 1)] if c_ == 7 else []))
                    ln3t_rd[slx_] = tPE.v
                    em.wait("vector", tPE, tPE.v)
                    em.op("vector", "tensor_copy",
                          fmx[:].rearrange("p (c t) -> p c t", c=8)[:, :, qt_ * 128:(qt_ + 1) * 128],
                          ptps[0][:].rearrange("p (c u) -> p c u", c=8),
                          incs=[(vV, 1)])
                    ptps_rd[0].append((vV, vV.v))

                for jt in range(4):
                    qt = qg * 4 + jt
                    slx = jt % 2
                    em.wait("sync", vV, xtB_rd[0])
                    em.op("sync", "dma_start", xtB[0][:],
                          xin[s, qt * 128:(qt + 1) * 128, :], incs=[(dXb, 16)])
                    xtB_tok[0] = dXb.v
                    em.wait("tensor", vV, tok_ofm)
                    em.wait("tensor", T["dWo"], tok_wo)
                    em.wait("tensor", vV, rb_rd)
                    for fo in range(2):
                        em.wait("tensor", sS, atps_rd)
                        for kc in range(8):
                            em.op("tensor", "matmul",
                                  at_ps[:],
                                  ofm[:, kc * 512 + jt * 128: kc * 512 + (jt + 1) * 128],
                                  wo_sb[:, kc * 1024 + fo * 512: kc * 1024 + (fo + 1) * 512],
                                  start=(kc == 0), stop=(kc == 7),
                                  incs=([(tPE, 1)] if kc == 7 else []))
                        em.wait("scalar", tPE, tPE.v)
                        if fo == 0:
                            em.wait("scalar", vV, atsb_rd[slx])
                        em.op("scalar", "activation",
                              atsb[slx][:, fo * 512:(fo + 1) * 512], at_ps[:],
                              AF.Identity, incs=[(sS, 1)])
                        atps_rd = sS.v
                    tok_wmm = tPE.v
                    if len(pending_ln3) >= 2:
                        emit_ln3_tp(pending_ln3.pop(0))
                    tok_atsb = sS.v
                    em.wait("vector", sS, stB_rd[slx])
                    r_ap, nm_ap, tok_nm, tok_r = _ln_stats(
                        em, T, stwB[slx], atsb[slx][:], T["scr"], 0,
                        [(sS, tok_atsb)], [(sS, tok_atsb)])
                    em.wait("vector", dXb, xtB_tok[0])
                    em.wait("vector", vV, tok_nm)
                    em.op("vector", "tensor_scalar", T["scr"][:], xtB[0][:],
                          nm_ap, None, ALU.add, incs=[(vV, 1)])
                    xtB_rd[0] = vV.v
                    em.wait("vector", sS, tok_r)
                    em.waits("vector", x2t_rd[slx])
                    x2t_rd[slx] = []
                    em.wait("vector", vV)
                    em.op("vector", "scalar_tensor_tensor", x2t[slx][:],
                          atsb[slx][:], r_ap, T["scr"][:], ALU.mult, ALU.add,
                          incs=[(vV, 1)])
                    atsb_rd[slx] = vV.v
                    tok_x2 = vV.v
                    em.wait("sync", vV, tok_x2)
                    em.op("sync", "dma_start", x2scr[s, qt * 128:(qt + 1) * 128, :],
                          x2t[slx][:], incs=[(dS[slx], 16)])
                    x2t_rd[slx].append((dS[slx], dS[slx].v))
                    r3_ap, nm3_ap, tok_nm3, tok_r3 = _ln_stats(
                        em, T, stwB[slx], x2t[slx][:], T["scr"], 8,
                        [(vV, tok_x2)], [(vV, tok_x2)])
                    em.wait("scalar", vV, tok_nm3)
                    em.wait("scalar", tPE, ln3t_rd[slx])
                    em.op("scalar", "activation", ln3t[slx][:], x2t[slx][:],
                          AF.Identity, bias=nm3_ap, scale=r3_ap, incs=[(sS, 1)])
                    stB_rd[slx] = sS.v
                    x2t_rd[slx].append((sS, sS.v))
                    pending_ln3.append((qt, slx, sS.v))
            while pending_ln3:
                emit_ln3_tp(pending_ln3.pop(0))
            em.flush()

    if "F" in _PHASES:
        _emit_ffn(nc, st, em, s, T)


def _emit_ffn(nc, st, em, s, T):
    xout, x2scr, dr = T["xout"], T["x2scr"], T["dr"]
    fmx = T["fmx"]
    dPro = T["dPro"]
    dF13 = [T["dF13_0"], T["dF13_1"]]
    dF2 = [T["dF2_0"], T["dF2_1"], T["dF2_2"], T["dF2_3"]]
    dFc = T["dFc"]
    dS = [T["dS0"], T["dS1"]]
    dXf = [T["dXf0"], T["dXf1"]]
    dO = [T["dO0"], T["dO1"]]
    tPE, vV, sS = T["tPE"], T["vV"], T["sS"]
    fm = lambda c, t0, n: fmx[:, c * 2048 + t0: c * 2048 + t0 + n]

    for _e in ("sync", "gpsimd", "tensor", "vector", "scalar"):
        em.wait(_e, T["dS0"])
        em.wait(_e, T["dS1"])
        em.wait(_e, T["dXb"])
        em.wait(_e, T["dWSt"])
        em.wait(_e, T["dWSt2"])
    with ExitStack() as fst:
        wbuf = [fst.enter_context(nc.sbuf_tensor(f"wF{s}_{i}", [128, 8192], BF16))
                for i in range(2)]
        w2buf = [fst.enter_context(nc.sbuf_tensor(f"w2F{s}_{i}", [128, 4096], BF16))
                 for i in range(4)]
        hsb = [fst.enter_context(nc.sbuf_tensor(f"hs{s}_{i}", [128, 4096], BF16))
               for i in range(4)]
        ysb = [fst.enter_context(nc.sbuf_tensor(f"ysb{s}_{j}", [128, 1024], F32))
               for j in range(JT_G)]
        h1sb = fst.enter_context(nc.sbuf_tensor(f"h1sb{s}", [128, 512], F32))
        hstb = fst.enter_context(nc.sbuf_tensor(f"hstF{s}", [128, 512], F32))
        gwork = fst.enter_context(nc.sbuf_tensor(f"gwork{s}", [128, 1024], F32))
        comb = fst.enter_context(nc.sbuf_tensor(f"comb{s}", [128, 128], BF16))
        cemb = [fst.enter_context(nc.sbuf_tensor(f"cemb{s}_{i}", [16, FG], BF16))
                for i in range(2)]
        x2tF = [fst.enter_context(nc.sbuf_tensor(f"x2F{s}_{i}", [128, 1024], F32))
                for i in range(2)]
        outt = [fst.enter_context(nc.sbuf_tensor(f"outt{s}_{i}", [128, 1024], F32))
                for i in range(2)]
        stwF = fst.enter_context(nc.sbuf_tensor(f"stF{s}", [128, 16], F32))
        hps = [fst.enter_context(nc.psum_tensor(f"hps{s}_{i}", [128, 512], F32))
               for i in range(2)]
        cbps = fst.enter_context(nc.psum_tensor(f"cbps{s}", [128, 512], F32))
        yps = [fst.enter_context(nc.psum_tensor(f"yps{s}_{i}", [128, 1024], F32))
               for i in range(2)]
        ctps = fst.enter_context(nc.psum_tensor(f"ctps{s}", [128, 1024], BF16))
        selm = fst.enter_context(nc.sbuf_tensor(f"selmF{s}", [16, 2048], BF16))
        be2s = fst.enter_context(nc.sbuf_tensor(f"be2F{s}", [16, 1024], BF16))
        bs2s = fst.enter_context(nc.sbuf_tensor(f"bs2F{s}", [1, 1024], BF16))
        em.op("gpsimd", "dma_start", selm[:], dr["csel"][:], incs=[(dFc, 16)])
        em.op("gpsimd", "dma_start", be2s[:], dr["be2"][:], incs=[(dFc, 16)])
        em.op("gpsimd", "dma_start", bs2s[:], dr["bs2"][:], incs=[(dFc, 16)])
        tok_fconst = dFc.v

        wbuf_rd = [0, 0]
        w2_rd = [0] * 4
        hs_rd = [0] * 4
        hps_rd = [[], []]
        h1sb_rd = [0]
        cb_rd = 0
        yps_rd = [0, 0]
        ysb_rd = [0] * JT_G
        cemb_rd = [0, 0]
        comb_rd = 0
        ctps_rd = 0
        x2F_rd = [0, 0]
        out_rd = [0, 0]
        x2_stores = (dS[0].v, dS[1].v)

        def emit_gate(g):
            nonlocal cb_rd, comb_rd, ctps_rd
            GW = gwork
            for jt in range(JT_G):
                qt = g * JT_G + jt
                em.wait("tensor", vV, cb_rd)
                if s == 0 and g == 0 and jt == 0:
                    em.wait("tensor", dPro, T["tok_prologue"])
                for kc in range(8):
                    em.op("tensor", "matmul", cbps[:, jt * 16:(jt + 1) * 16],
                          fm(kc, qt * 128, 128), T["gwt"][:, kc * 16:(kc + 1) * 16],
                          start=(kc == 0), stop=(kc == 7),
                          incs=([(tPE, 1)] if kc == 7 else []))
            tok_gmm = tPE.v
            em.wait("vector", tPE, tok_gmm)
            em.op("vector", "tensor_copy", GW[:, 0:128], cbps[:, 0:128],
                  incs=[(vV, 1)])
            cb_rd = vV.v
            em.wait("scalar", vV, vV.v)
            em.op("scalar", "activation", GW[:, 0:128], GW[:, 0:128], AF.Exp,
                  incs=[(sS, 1)])
            gv = lambda off: GW[:, off:off + 128].rearrange("p (j i) -> p j i", i=16)
            bc8 = lambda off: bass.AP(gwork, off, [[1024, 128], [1, 8], [0, 16]])
            bc8g = lambda off: bass.AP(gwork, off, [[1024, 128], [1, 8], [0, 4]])
            em.wait("vector", sS, sS.v)
            em.op("vector", "tensor_reduce", GW[:, 608:616],
                  gv(0), AX.X, ALU.add, incs=[(vV, 1)])
            em.wait("vector", vV)
            em.op("vector", "reciprocal", GW[:, 616:624], GW[:, 608:616],
                  incs=[(vV, 1)])
            em.wait("vector", vV)
            em.op("vector", "tensor_tensor", gv(128), gv(0), bc8(616),
                  ALU.mult, incs=[(vV, 1)])
            em.wait("vector", vV)
            em.op("vector", "tensor_tensor", gv(256), gv(128),
                  bass.AP(T["gb16"], 0, [[16, 128], [0, 8], [1, 16]]),
                  ALU.add, incs=[(vV, 1)])
            em.wait("vector", vV)
            pv = lambda i: bass.AP(gwork, 256 + i, [[1024, 128], [16, 8], [4, 4]])
            pi = 0
            for i in range(4):
                for j in range(i + 1, 4):
                    em.op("vector", "tensor_tensor",
                          GW[:, 384 + pi * 32:384 + pi * 32 + 32].rearrange(
                              "p (j i) -> p j i", i=4),
                          pv(i), pv(j), ALU.add, incs=[(vV, 1)])
                    pi += 1
            em.wait("vector", vV)
            em.op("vector", "tensor_tensor", GW[:, 576:608], GW[:, 384:416],
                  GW[:, 416:448], ALU.max, incs=[(vV, 1)])
            for pi in range(2, 6):
                em.wait("vector", vV)
                em.op("vector", "tensor_tensor", GW[:, 576:608], GW[:, 576:608],
                      GW[:, 384 + pi * 32:416 + pi * 32], ALU.max, incs=[(vV, 1)])
            gsv = GW[:, 576:608].rearrange("p (j i) -> p j i", i=4)
            em.wait("vector", vV)
            em.op("vector", "tensor_reduce", GW[:, 624:632], gsv, AX.X, ALU.max,
                  incs=[(vV, 1)])
            em.wait("vector", vV)
            em.op("vector", "tensor_tensor", GW[:, 640:672].rearrange(
                      "p (j i) -> p j i", i=4), gsv, bc8g(624),
                  ALU.is_ge, incs=[(vV, 1)])
            em.wait("vector", vV)
            em.op("vector", "scalar_tensor_tensor", GW[:, 672:704],
                  GW[:, 640:672], NBIG, GW[:, 576:608], ALU.mult, ALU.add,
                  incs=[(vV, 1)])
            em.wait("vector", vV)
            em.op("vector", "tensor_reduce", GW[:, 632:640],
                  GW[:, 672:704].rearrange("p (j i) -> p j i", i=4), AX.X,
                  ALU.max, incs=[(vV, 1)])
            em.wait("vector", vV)
            em.op("vector", "tensor_tensor", GW[:, 640:672].rearrange(
                      "p (j i) -> p j i", i=4), gsv, bc8g(632),
                  ALU.is_ge, incs=[(vV, 1)])
            em.wait("vector", vV)
            em.op("vector", "tensor_copy",
                  GW[:, 704:832].rearrange("p (j g i) -> p j g i", g=4, i=4),
                  bass.AP(gwork, 640, [[1024, 128], [4, 8], [1, 4], [0, 4]]),
                  incs=[(vV, 1)])
            em.wait("vector", vV)
            em.op("vector", "tensor_scalar", GW[:, 704:832], GW[:, 704:832],
                  1.0, BIG, ALU.subtract, ALU.mult, incs=[(vV, 1)])
            em.wait("vector", vV)
            em.op("vector", "tensor_tensor", GW[:, 384:512], GW[:, 256:384],
                  GW[:, 704:832], ALU.add, incs=[(vV, 1)])
            em.wait("vector", vV)
            em.op("vector", "tensor_reduce", GW[:, 624:632], gv(384), AX.X,
                  ALU.max, incs=[(vV, 1)])
            em.wait("vector", vV)
            em.op("vector", "tensor_tensor", gv(512), gv(384), bc8(624),
                  ALU.is_ge, incs=[(vV, 1)])
            em.wait("vector", vV)
            em.op("vector", "scalar_tensor_tensor", GW[:, 640:768],
                  GW[:, 512:640], NBIG, GW[:, 384:512], ALU.mult, ALU.add,
                  incs=[(vV, 1)])
            em.wait("vector", vV)
            em.op("vector", "tensor_reduce", GW[:, 632:640], gv(640), AX.X,
                  ALU.max, incs=[(vV, 1)])
            em.wait("vector", vV)
            em.op("vector", "tensor_tensor", gv(512), gv(384), bc8(632),
                  ALU.is_ge, incs=[(vV, 1)])
            em.wait("vector", vV)
            em.wait("vector", tPE, comb_rd)
            em.op("vector", "tensor_tensor", comb[:], GW[:, 128:256],
                  GW[:, 512:640], ALU.mult, incs=[(vV, 1)])
            tok_comb = vV.v
            for jt in range(JT_G):
                em.wait("tensor", vV, tok_comb)
                em.wait("tensor", vV, ctps_rd)
                em.op("tensor", "transpose", ctps[0:16, jt * 128:(jt + 1) * 128],
                      comb[:, jt * 16:(jt + 1) * 16], T["ident"][:],
                      incs=[(tPE, 1)])
            comb_rd = tPE.v
            em.wait("vector", tPE, tPE.v)
            em.wait("vector", tPE, cemb_rd[g])
            em.op("vector", "tensor_copy", cemb[g][:],
                  ctps[0:16, 0:1024], incs=[(vV, 1)])
            ctps_rd = vV.v
            cemb_tok[g] = vV.v

        cemb_tok = [0, 0]
        emit_gate(0)
        emit_gate(1)

        for g in range(NFG):


            for ci, chunk in enumerate(CHUNKS):
                for e in chunk:
                    wsl = e % 2
                    w2sl = (e % 4) if e < 16 else (e - 16)
                    if s == 0:
                        em.wait("gpsimd", tPE, wbuf_rd[wsl])
                        em.op("gpsimd", "dma_start", wbuf[wsl][:],
                              dr["we13"][e, :, :], incs=[(dF13[wsl], 16)])
                        tok_w13 = (dF13[wsl], dF13[wsl].v)
                        em.wait("gpsimd", tPE, w2_rd[w2sl])
                        em.op("gpsimd", "dma_start", w2buf[w2sl][:],
                              dr["we2"][e, :, :], incs=[(dF2[w2sl], 16)])
                    else:
                        em.wait("gpsimd", T["dWSt"], T["tok_wbf"][0])
                        em.wait("gpsimd", T["dWSt2"], T["tok_wbf"][1])
                        em.wait("gpsimd", tPE, wbuf_rd[wsl])
                        em.op("gpsimd", "dma_start", wbuf[wsl][:],
                              dr["wbf13"][e, :, :], incs=[(dF13[wsl], 16)])
                        tok_w13 = (dF13[wsl], dF13[wsl].v)
                        em.wait("gpsimd", tPE, w2_rd[w2sl])
                        em.op("gpsimd", "dma_start", w2buf[w2sl][:],
                              dr["wbf2"][e, :, :], incs=[(dF2[w2sl], 16)])
                    for tch in range(FG // 512):
                        if e < 16:
                            em.wait("tensor", vV, cb_rd)
                            em.wait("tensor", vV, cemb_tok[g])
                            em.wait("tensor", dFc, tok_fconst)
                            em.op("tensor", "matmul", cbps[:],
                                  selm[:, e * 128:(e + 1) * 128],
                                  cemb[g][:, tch * 512:(tch + 1) * 512],
                                  start=True, stop=True, incs=[(tPE, 1)])
                            tok_cb = tPE.v
                        for ic in range(4):
                            s1, s3 = 0, 1
                            em.wait("tensor", tok_w13[0], tok_w13[1])
                            em.waits("tensor", hps_rd[s1])
                            hps_rd[s1] = []
                            for kc in range(8):
                                em.op("tensor", "matmul", hps[s1][:],
                                      wbuf[wsl][:, kc * 1024 + ic * 128: kc * 1024 + ic * 128 + 128],
                                      fm(kc, g * FG + tch * 512, 512),
                                      start=(kc == 0), stop=(kc == 7),
                                      incs=([(tPE, 1)] if kc == 7 else []))
                            tok_h1 = tPE.v
                            em.waits("tensor", hps_rd[s3])
                            hps_rd[s3] = []
                            for kc in range(8):
                                em.op("tensor", "matmul", hps[s3][:],
                                      wbuf[wsl][:, kc * 1024 + 512 + ic * 128: kc * 1024 + 512 + ic * 128 + 128],
                                      fm(kc, g * FG + tch * 512, 512),
                                      start=(kc == 0), stop=(kc == 7),
                                      incs=([(tPE, 1)] if kc == 7 else []))
                            tok_h3 = tPE.v
                            wbuf_rd[wsl] = tok_h3
                            em.wait("scalar", tPE, tok_h1)
                            em.wait("scalar", vV, h1sb_rd[0])
                            em.op("scalar", "activation", h1sb[:], hps[s1][:],
                                  AF.Silu,
                                  bias=T["bia1"][:, e * 4 + ic:e * 4 + ic + 1],
                                  incs=[(sS, 1)])
                            hps_rd[s1].append((sS, sS.v))
                            em.wait("vector", tPE, tok_h3)
                            em.wait("vector", sS, sS.v)
                            em.wait("vector", vV)
                            em.op("vector", "scalar_tensor_tensor", hstb[:],
                                  hps[s3][:],
                                  T["bia3"][:, e * 4 + ic:e * 4 + ic + 1],
                                  h1sb[:], ALU.add, ALU.mult, incs=[(vV, 1)])
                            hps_rd[s3].append((vV, vV.v))
                            h1sb_rd[0] = vV.v
                            em.wait("vector", vV)
                            em.wait("vector", tPE, hs_rd[w2sl])
                            dsth = hsb[w2sl][:, ic * 1024 + tch * 512: ic * 1024 + (tch + 1) * 512]
                            if e < 16:
                                em.wait("vector", tPE, tok_cb)
                                em.op("vector", "tensor_tensor", dsth, hstb[:],
                                      cbps[:], ALU.mult, incs=[(vV, 1)])
                                cb_rd = vV.v
                            else:
                                em.op("vector", "tensor_copy", dsth, hstb[:],
                                      incs=[(vV, 1)])
                tok_hs_all = vV.v
                tok_w2_all = [(d, d.v) for d in dF2]
                nch = len(chunk)
                for jt in range(JT_G):
                    ysl = jt % 2
                    em.wait("tensor", vV, tok_hs_all)
                    em.waits("tensor", tok_w2_all)
                    em.wait("tensor", vV, yps_rd[ysl])
                    if ci == 0:
                        em.wait("tensor", vV, cemb_tok[g])
                        em.wait("tensor", dFc, tok_fconst)
                        for fo in range(2):
                            em.op("tensor", "matmul",
                                  yps[ysl][:, fo * 512:(fo + 1) * 512],
                                  cemb[g][:, jt * 128:(jt + 1) * 128],
                                  be2s[:, fo * 512:(fo + 1) * 512],
                                  start=True, stop=False)
                            em.op("tensor", "matmul",
                                  yps[ysl][:, fo * 512:(fo + 1) * 512],
                                  T["onec"][:],
                                  bs2s[:, fo * 512:(fo + 1) * 512],
                                  start=False, stop=False)
                    for i4, e in enumerate(chunk):
                        w2sl = (e % 4) if e < 16 else (e - 16)
                        for ic in range(4):
                            st_flag = (ci != 0 and i4 == 0 and ic == 0)
                            for fo in range(2):
                                em.op("tensor", "matmul",
                                      yps[ysl][:, fo * 512:(fo + 1) * 512],
                                      hsb[w2sl][:, ic * 1024 + jt * 128: ic * 1024 + jt * 128 + 128],
                                      w2buf[w2sl][:, ic * 1024 + fo * 512: ic * 1024 + (fo + 1) * 512],
                                      start=st_flag,
                                      stop=(i4 == nch - 1 and ic == 3),
                                      incs=([(tPE, 1)] if (i4 == nch - 1 and ic == 3 and fo == 1) else []))
                    tok_y = tPE.v
                    cemb_rd[g] = tok_y
                    for e in chunk:
                        w2sl = (e % 4) if e < 16 else (e - 16)
                        w2_rd[w2sl] = tok_y
                        hs_rd[w2sl] = tok_y
                    em.wait("vector", tPE, tok_y)
                    em.wait("vector", sS, ysb_rd[jt])
                    if ci == 0:
                        em.op("vector", "tensor_copy", ysb[jt][:], yps[ysl][:],
                              incs=[(vV, 1)])
                    else:
                        em.op("vector", "tensor_tensor", ysb[jt][:], ysb[jt][:],
                              yps[ysl][:], ALU.add, incs=[(vV, 1)])
                    yps_rd[ysl] = vV.v

            for jt in range(JT_G):
                qt = g * JT_G + jt
                slx = jt % 2
                em.wait("sync", T["dS0"], x2_stores[0])
                em.wait("sync", T["dS1"], x2_stores[1])
                em.wait("sync", vV, x2F_rd[slx])
                em.op("sync", "dma_start", x2tF[slx][:],
                      x2scr[s, qt * 128:(qt + 1) * 128, :], incs=[(dXf[slx], 16)])
                tok_x2l = dXf[slx].v
                r_ap, nm_ap, tok_nm, tok_r = _ln_stats(
                    em, T, stwF, ysb[jt][:], T["scr"], 0,
                    [(vV, vV.v)], [(vV, vV.v)])
                ysb_rd[jt] = sS.v
                em.wait("vector", dXf[slx], tok_x2l)
                em.wait("vector", vV, tok_nm)
                em.op("vector", "tensor_scalar", T["scr"][:], x2tF[slx][:],
                      nm_ap, None, ALU.add, incs=[(vV, 1)])
                x2F_rd[slx] = vV.v
                em.wait("vector", sS, tok_r)
                em.wait("vector", dO[slx], out_rd[slx])
                em.wait("vector", vV)
                em.op("vector", "scalar_tensor_tensor", outt[slx][:], ysb[jt][:],
                      r_ap, T["scr"][:], ALU.mult, ALU.add, incs=[(vV, 1)])
                tok_out = vV.v
                em.wait("sync", vV, tok_out)
                em.op("sync", "dma_start", xout[s, qt * 128:(qt + 1) * 128, :],
                      outt[slx][:], incs=[(dO[slx], 16)])
                out_rd[slx] = dO[slx].v
        em.flush()


_NC_CACHE = None


def _get_nc():
    global _NC_CACHE
    if _NC_CACHE is None:
        _NC_CACHE = build_program()
    return _NC_CACHE


def _layout_weights(inputs):
    f = np.float32
    out = {}
    chunk_lhs = lambda w: np.ascontiguousarray(
        w.reshape(8, 128, -1).transpose(1, 0, 2).reshape(128, -1))
    Wkv = np.asarray(inputs["L0_attn_Wkv"], f)
    out["wq"] = chunk_lhs(np.asarray(inputs["L0_attn_Wq"], f))
    out["wk"] = chunk_lhs(np.ascontiguousarray(Wkv[:, :D]))
    out["wv"] = chunk_lhs(np.ascontiguousarray(Wkv[:, D:]))
    out["wo"] = chunk_lhs(np.asarray(inputs["L0_attn_Wo"], f))
    gw = np.asarray(inputs["L0_ffn_gate_w"], f)
    out["gwt"] = chunk_lhs(np.ascontiguousarray(gw.T))
    out["gb16"] = np.ascontiguousarray(
        np.broadcast_to(np.asarray(inputs["L0_ffn_gate_b"], f)[None, :], (128, 16)))
    We1 = np.asarray(inputs["L0_ffn_We1"], f)
    We3 = np.asarray(inputs["L0_ffn_We3"], f)
    We2 = np.asarray(inputs["L0_ffn_We2"], f)
    ws1 = np.asarray(inputs["L0_ffn_ws1"], f)
    ws3 = np.asarray(inputs["L0_ffn_ws3"], f)
    ws2 = np.asarray(inputs["L0_ffn_ws2"], f)
    w13 = np.empty((NE, 128, 8192), f)
    w2 = np.empty((NE, 128, 4096), f)
    for e in range(NE):
        if e < 16:
            a, b, c = We1[e], We3[e], We2[e]
        else:
            hf = e - 16
            a = ws1[:, hf * 512:(hf + 1) * 512]
            b = ws3[:, hf * 512:(hf + 1) * 512]
            c = ws2[hf * 512:(hf + 1) * 512, :]
        ar = np.ascontiguousarray(a).reshape(8, 128, 512)
        br = np.ascontiguousarray(b).reshape(8, 128, 512)
        w13[e] = np.concatenate([ar, br], axis=2).transpose(1, 0, 2).reshape(128, 8192)
        w2[e] = np.ascontiguousarray(c).reshape(4, 128, 1024).transpose(1, 0, 2).reshape(128, 4096)
    out["we13"] = w13
    out["we2"] = w2
    lay_b = lambda be, bsh: np.concatenate(
        [be.reshape(16, 4, 128).transpose(2, 0, 1).reshape(128, 64),
         bsh.reshape(2, 4, 128).transpose(2, 0, 1).reshape(128, 8)], axis=1)
    out["bia1"] = np.ascontiguousarray(
        lay_b(np.asarray(inputs["L0_ffn_be1"], f), np.asarray(inputs["L0_ffn_bs1"], f)))
    out["bia3"] = np.ascontiguousarray(
        lay_b(np.asarray(inputs["L0_ffn_be3"], f), np.asarray(inputs["L0_ffn_bs3"], f)))
    out["be2"] = np.asarray(inputs["L0_ffn_be2"], f)
    out["bs2"] = np.asarray(inputs["L0_ffn_bs2"], f).reshape(1, 1024)
    out["cident"] = np.eye(128, dtype=f)
    kl = np.arange(128)[:, None]
    ql = np.arange(512)[None, :]
    out["cmask"] = np.concatenate(
        [np.where(ql >= r * 128 + kl, 0.0, -50000.0).astype(f) for r in range(4)],
        axis=1)
    sel = np.zeros((16, 2048), f)
    for e in range(16):
        sel[e, e * 128:(e + 1) * 128] = 1.0
    out["csel"] = sel
    return out


def kernel(**inputs):
    nc = _get_nc()
    w = _layout_weights(inputs)
    x = np.asarray(inputs["L0_attn_x"], np.float32)
    in_maps = []
    for c in range(8):
        m = dict(w)
        m["xin"] = np.ascontiguousarray(x[c * NSEQ:(c + 1) * NSEQ])
        in_maps.append(m)
    res = run_bass_kernel_spmd(nc, in_maps, core_ids=list(range(8)))
    return np.concatenate([np.asarray(r["xout"], np.float32) for r in res.results],
                          axis=0)


if __name__ == "__main__":
    import time
    t0 = time.time()
    nc = build_program()
    n = sum(1 for _ in nc.all_instructions())
    print(f"built in {time.time()-t0:.1f}s, {n} instructions")


# version 17
# speedup vs baseline: 1.1546x; 1.1546x over previous
import numpy as np
import ml_dtypes
from contextlib import ExitStack

import concourse.bass as bass
import concourse.mybir as mybir
from concourse.bass_utils import run_bass_kernel_spmd

F32 = mybir.dt.float32
BF16 = mybir.dt.bfloat16
AF = mybir.ActivationFunctionType
ALU = mybir.AluOpType
AX = mybir.AxisListType

NSEQ = 4
S = 2048
D = 1024
H = 16
HD = 64
NT = S // 128
NQG = 4
NE = 18
FG = 1024
NFG = S // FG
JT_G = FG // 128
INV_D = 1.0 / 1024.0
EPS = np.float32(1e-5)
NEG1 = -1.0
NEG1_S = np.float32(-1.0)
NEGH = np.float32(-0.5)
QSCALE = np.float32(0.125)
EXPB = np.float32(-30.0)
BIG = 1e9
NBIG = -1e9

CHUNKS = [(0, 1, 2, 3), (4, 5, 6, 7), (8, 9, 10, 11), (12, 13, 14, 15), (16, 17)]

_PHASES = "ABF"
_NSEQ_RUN = NSEQ
_B_LIMIT = "full"


class Sem:
    def __init__(self, stack, nc, name):
        self.h = stack.enter_context(nc.semaphore(name))
        self.v = 0


class Emitter:

    def __init__(self, nc):
        self.nc = nc
        self.ops = []
        self.waited = {}

    def op(self, eng, method, *args, incs=(), **kwargs):
        self.ops.append((eng, method, args, kwargs, list(incs)))
        for sem, k in incs:
            sem.v += k

    def wait(self, eng, sem, val=None):
        v = sem.v if val is None else val
        key = (eng, sem.h.num)
        if v > 0 and v > self.waited.get(key, 0):
            self.waited[key] = v
            self.ops.append((eng, "wait_ge", (sem.h, v), {}, []))

    def waits(self, eng, deps):
        for sem, tok in deps:
            self.wait(eng, sem, tok)

    def flush(self):
        ops = self.ops
        self.ops = []
        if not ops:
            return
        with self.nc.Block() as blk:
            for name in ("sync", "gpsimd", "tensor", "vector", "scalar"):
                mine = [o for o in ops if o[0] == name]
                if not mine:
                    continue

                def body(eng, mine=mine):
                    for (_, method, args, kwargs, incs) in mine:
                        if method == "wait_ge":
                            eng.wait_ge(*args)
                            continue
                        bi = getattr(eng, method)(*args, **kwargs)
                        for sem, k in incs:
                            bi.then_inc(sem.h, k)

                getattr(blk, name)(body)


def _ln_stats(em, T, stw, src_ap, sqd, col0, deps_v, deps_s):
    vV, sS = T["vV"], T["sS"]
    c = lambda i: stw[:, col0 + i:col0 + i + 1]
    em.waits("vector", deps_v)
    em.waits("scalar", deps_s)
    em.op("vector", "reduce_sum", c(0), src_ap, axis=AX.X, incs=[(vV, 1)])
    em.wait("scalar", vV, vV.v)
    em.op("scalar", "activation", sqd[:], src_ap, AF.Square,
          accum_out=c(1), incs=[(sS, 1)])
    tok_sq = sS.v
    em.wait("vector", vV)
    em.op("vector", "tensor_scalar", c(2), c(0), INV_D, None, ALU.mult, incs=[(vV, 1)])
    em.wait("vector", sS, tok_sq)
    em.op("vector", "tensor_scalar", c(3), c(1), INV_D, None, ALU.mult, incs=[(vV, 1)])
    em.wait("vector", vV)
    em.op("vector", "tensor_scalar", c(4), c(2), c(2), None, ALU.mult, incs=[(vV, 1)])
    em.wait("vector", vV)
    em.op("vector", "tensor_tensor", c(5), c(3), c(4), ALU.subtract, incs=[(vV, 1)])
    em.wait("scalar", vV, vV.v)
    em.op("scalar", "activation", c(6), c(5), AF.Ln, bias=EPS, incs=[(sS, 1)])
    em.wait("scalar", sS)
    em.op("scalar", "activation", c(5), c(6), AF.Exp, scale=NEGH, incs=[(sS, 1)])
    tok_r = sS.v
    em.wait("vector", sS, tok_r)
    em.op("vector", "tensor_scalar", c(6), c(2), c(5), NEG1, ALU.mult, ALU.mult,
          incs=[(vV, 1)])
    tok_nm = vV.v
    return c(5), c(6), tok_nm, tok_r


def build_program():
    nc = bass.Bass("TRN2", target_bir_lowering=False)

    xin = nc.dram_tensor("xin", [NSEQ, S, D], F32, kind="ExternalInput").ap()
    dr = {}
    for nm, shp in [("wq", [128, 8192]), ("wk", [128, 8192]), ("wv", [128, 8192]),
                    ("wo", [128, 8192]), ("gwt", [128, 128]), ("gb16", [128, 16]),
                    ("bia1", [128, 72]), ("bia3", [128, 72]), ("be2", [16, 1024]),
                    ("bs2", [1, 1024]), ("we13", [NE, 128, 8192]),
                    ("we2", [NE, 128, 4096]), ("cident", [128, 128]),
                    ("cmask", [128, 2048]), ("csel", [16, 2048])]:
        dr[nm] = nc.dram_tensor(nm, shp, F32, kind="ExternalInput").ap()
    xout = nc.dram_tensor("xout", [NSEQ, S, D], F32, kind="ExternalOutput").ap()
    x2scr = nc.dram_tensor("x2scr", [NSEQ, S, D], F32, kind="Internal").ap()

    st = ExitStack()
    with st:
        em = Emitter(nc)
        T = {"nc": nc, "em": em, "xin": xin, "xout": xout, "x2scr": x2scr, "dr": dr}
        for nm in ("dPro", "dWa0", "dWa1", "dWo", "dXa0", "dXa1", "dXb",
                   "dF13_0", "dF13_1", "dF2_0", "dF2_1", "dF2_2", "dF2_3",
                   "dFc", "dS0", "dS1", "dXf0", "dXf1", "dO0", "dO1",
                   "tPE", "vV", "sS"):
            T[nm] = Sem(st, nc, nm)
        dPro = T["dPro"]

        sb = lambda name, shp, dt=BF16: st.enter_context(nc.sbuf_tensor(name, shp, dt))
        T["ident"] = sb("ident", [128, 128])
        T["masks"] = sb("masks", [128, 2048])
        T["gwt"] = sb("gwtS", [128, 128])
        T["gb16"] = sb("gb16S", [128, 16], F32)
        T["bia1"] = sb("bia1S", [128, 72], F32)
        T["bia3"] = sb("bia3S", [128, 72], F32)
        T["onec"] = sb("onec", [1, 128])
        T["fmx"] = sb("fmx", [128, 8 * 2048])
        T["scr"] = sb("scr", [128, 1024], F32)
        T["ones1f"] = sb("ones1f", [1, 64], F32)

        for dst, srcnm in [(T["ident"], "cident"), (T["masks"], "cmask"),
                           (T["gwt"], "gwt"),
                           (T["gb16"], "gb16"), (T["bia1"], "bia1"),
                           (T["bia3"], "bia3")]:
            em.op("gpsimd", "dma_start", dst[:], dr[srcnm][:], incs=[(dPro, 16)])
        em.op("vector", "memset", T["onec"][:], 1.0)
        em.op("vector", "memset", T["ones1f"][:], 1.0, incs=[(T["vV"], 1)])
        T["tok_prologue"] = dPro.v
        em.flush()

        for s in range(_NSEQ_RUN):
            _emit_seq(nc, st, em, s, T)

        em.wait("sync", T["dO0"])
        em.wait("sync", T["dO1"])
        em.flush()

    nc.finalize()
    return nc


def _emit_seq(nc, st, em, s, T):
    xin, xout, x2scr, dr = T["xin"], T["xout"], T["x2scr"], T["dr"]
    fmx, ident = T["fmx"], T["ident"]
    dPro = T["dPro"]
    dWa = [T["dWa0"], T["dWa1"]]
    dXa = [T["dXa0"], T["dXa1"]]
    dXb = T["dXb"]
    dS = [T["dS0"], T["dS1"]]
    tPE, vV, sS = T["tPE"], T["vV"], T["sS"]
    fm = lambda c, t0, n: fmx[:, c * 2048 + t0: c * 2048 + t0 + n]

    if "A" not in _PHASES:
        return
    for _e in ("sync", "gpsimd", "tensor", "vector", "scalar"):
        em.wait(_e, T["dO0"])
        em.wait(_e, T["dO1"])
        em.wait(_e, T["dXf0"])
        em.wait(_e, T["dXf1"])
    with ExitStack() as qst:
        q_fm = qst.enter_context(nc.sbuf_tensor(f"q_fm{s}", [128, 8 * 2048], BF16))
        k_fm = qst.enter_context(nc.sbuf_tensor(f"k_fm{s}", [128, 8 * 2048], BF16))
        v_tm = qst.enter_context(nc.sbuf_tensor(f"v_tm{s}", [128, 16 * 1040], BF16))

        with ExitStack() as ast:
            xt = [ast.enter_context(nc.sbuf_tensor(f"xtA{s}_{i}", [128, 1024], F32))
                  for i in range(2)]
            ln1t = [ast.enter_context(nc.sbuf_tensor(f"ln1t{s}_{i}", [128, 1024], BF16))
                    for i in range(2)]
            stw = [ast.enter_context(nc.sbuf_tensor(f"stA{s}_{i}", [128, 16], F32))
                   for i in range(2)]
            wbufA = [ast.enter_context(nc.sbuf_tensor(f"wA{s}_{i}", [128, 8192], BF16))
                     for i in range(2)]
            ps = [ast.enter_context(nc.psum_tensor(f"psA{s}_{i}", [128, 512], F32))
                  for i in range(2)]
            tp = [ast.enter_context(nc.psum_tensor(f"tpA{s}_{i}", [128, 1024], BF16))
                  for i in range(2)]

            em.op("gpsimd", "dma_start", wbufA[0][:], dr["wk"][:], incs=[(dWa[0], 16)])
            tok_wk = (dWa[0], dWa[0].v)
            em.op("gpsimd", "dma_start", wbufA[1][:], dr["wv"][:], incs=[(dWa[1], 16)])
            tok_wv = (dWa[1], dWa[1].v)

            xt_vrd = [0, 0]
            xt_srd = [0, 0]
            tp_rd = [0, 0]
            ln1t_rd = [0, 0]
            st_rd = [0, 0]
            ln1_tok = [0] * NT

            ps_rd = [0, 0]
            ridx = [0]

            def emit_tile_ln(qt):
                sl = qt % 2
                em.wait("sync", vV, xt_vrd[sl])
                em.wait("sync", sS, xt_srd[sl])
                em.op("sync", "dma_start", xt[sl][:],
                      xin[s, qt * 128:(qt + 1) * 128, :], incs=[(dXa[sl], 16)])
                tok_x = dXa[sl].v
                em.wait("vector", sS, st_rd[sl])
                r_ap, nm_ap, tok_nm, tok_r = _ln_stats(
                    em, T, stw[sl], xt[sl][:], T["scr"], 0,
                    [(dXa[sl], tok_x)], [(dXa[sl], tok_x)])
                em.wait("scalar", tPE, ln1t_rd[sl])
                em.wait("scalar", vV, tok_nm)
                em.op("scalar", "activation", ln1t[sl][:], xt[sl][:], AF.Identity,
                      bias=nm_ap, scale=r_ap, incs=[(sS, 1)])
                st_rd[sl] = sS.v
                xt_srd[sl] = sS.v
                xt_vrd[sl] = tok_nm
                em.wait("tensor", sS, sS.v)
                em.wait("tensor", vV, tp_rd[sl])
                if qt == 0 and s == 0:
                    em.wait("tensor", dPro, T["tok_prologue"])
                for c_ in range(8):
                    em.op("tensor", "transpose", tp[sl][:, c_ * 128:(c_ + 1) * 128],
                          ln1t[sl][:, c_ * 128:(c_ + 1) * 128], ident[:],
                          incs=([(tPE, 1)] if c_ == 7 else []))
                ln1t_rd[sl] = tPE.v
                em.wait("vector", tPE, tPE.v)
                em.op("vector", "tensor_copy",
                      fmx[:].rearrange("p (c t) -> p c t", c=8)[:, :, qt * 128:(qt + 1) * 128],
                      tp[sl][:].rearrange("p (c u) -> p c u", c=8),
                      incs=[(vV, 1)])
                tp_rd[sl] = vV.v
                ln1_tok[qt] = vV.v

            def proj_tch(dst, wsb, tok_w, scale, tch, fo_list):
                em.wait("tensor", tok_w[0], tok_w[1])
                em.wait("tensor", vV, ln1_tok[tch * 4 + 3])
                for fo in fo_list:
                    sl = ridx[0] % 2
                    ridx[0] += 1
                    em.wait("tensor", sS, ps_rd[sl])
                    for kc in range(8):
                        em.op("tensor", "matmul", ps[sl][:],
                              wsb[:, kc * 1024 + fo * 128: kc * 1024 + fo * 128 + 128],
                              fm(kc, tch * 512, 512),
                              start=(kc == 0), stop=(kc == 7),
                              incs=([(tPE, 1)] if kc == 7 else []))
                    em.wait("scalar", tPE, tPE.v)
                    em.op("scalar", "activation",
                          dst[:, fo * 2048 + tch * 512: fo * 2048 + (tch + 1) * 512],
                          ps[sl][:], AF.Identity, scale=scale, incs=[(sS, 1)])
                    ps_rd[sl] = sS.v
                return tPE.v

            def proj_fm(dst, wsb, tok_w, scale):
                for fo in range(8):
                    for tch in range(4):
                        proj_tch(dst, wsb, tok_w, scale, tch, [fo])
                return tPE.v

            em.op("vector", "memset",
                  v_tm[:].rearrange("p (a x) -> p a x", x=65)[:, :, 64:65], 1.0,
                  incs=[(vV, 1)])
            tok_vones = vV.v

            def emit_v_qt(qt):
                em.wait("tensor", tok_wv[0], tok_wv[1])
                em.wait("tensor", vV, tok_vones)
                em.wait("tensor", vV, ln1_tok[qt])
                for fo in range(2):
                    sl = ridx[0] % 2
                    ridx[0] += 1
                    em.wait("tensor", sS, ps_rd[sl])
                    for kc in range(8):
                        em.op("tensor", "matmul", ps[sl][:],
                              fm(kc, qt * 128, 128),
                              wbufA[1][:, kc * 1024 + fo * 512: kc * 1024 + (fo + 1) * 512],
                              start=(kc == 0), stop=(kc == 7),
                              incs=([(tPE, 1)] if kc == 7 else []))
                    em.wait("scalar", tPE, tPE.v)
                    dstv = v_tm[:, qt * 1040 + fo * 520: qt * 1040 + (fo + 1) * 520]
                    em.op("scalar", "activation",
                          dstv.rearrange("p (h x) -> p h x", x=65)[:, :, 0:64],
                          ps[sl][:].rearrange("p (h x) -> p h x", x=64),
                          AF.Identity, incs=[(sS, 1)])
                    ps_rd[sl] = sS.v

            for tch in range(4):
                for qt in range(tch * 4, tch * 4 + 4):
                    emit_tile_ln(qt)
                proj_tch(k_fm, wbufA[0], tok_wk, np.float32(1.0), tch,
                         list(range(8)))
                for qt in range(tch * 4, tch * 4 + 4):
                    emit_v_qt(qt)
            tok_kmm = tPE.v
            em.wait("gpsimd", tPE, tok_kmm)
            em.op("gpsimd", "dma_start", wbufA[0][:], dr["wq"][:], incs=[(dWa[0], 16)])
            tok_wq = (dWa[0], dWa[0].v)
            proj_fm(q_fm, wbufA[0], tok_wq, QSCALE)
            em.flush()

        if "B" not in _PHASES:
            em.flush()
            return
        for _e in ("sync", "gpsimd", "tensor", "vector", "scalar"):
            em.wait(_e, T["dXa0"])
            em.wait(_e, T["dXa1"])
            em.wait(_e, T["dWa0"])
            em.wait(_e, T["dWa1"])
        with ExitStack() as bst:
            wo_sb = bst.enter_context(nc.sbuf_tensor(f"wo_sb{s}", [128, 8192], BF16))
            pt_sb = [bst.enter_context(nc.sbuf_tensor(f"ptsb{s}_{i}", [128, 1024], BF16))
                     for i in range(2)]
            o_sb = [bst.enter_context(nc.sbuf_tensor(f"osb{s}_{i}", [128, 512], F32))
                    for i in range(2)]
            r_row = [bst.enter_context(nc.sbuf_tensor(f"rrow{s}_{i}", [1, 512], BF16))
                     for i in range(2)]
            lrow32 = bst.enter_context(nc.sbuf_tensor(f"lrow{s}", [1, 512], F32))
            ofm = bst.enter_context(nc.sbuf_tensor(f"ofm{s}", [128, 8 * 512], BF16))
            atsb = [bst.enter_context(nc.sbuf_tensor(f"atsb{s}_{i}", [128, 1024], F32))
                    for i in range(2)]
            xtB = [bst.enter_context(nc.sbuf_tensor(f"xtB{s}", [128, 1024], F32))]
            x2t = [bst.enter_context(nc.sbuf_tensor(f"x2t{s}_{i}", [128, 1024], F32))
                   for i in range(2)]
            ln3t = [bst.enter_context(nc.sbuf_tensor(f"ln3t{s}_{i}", [128, 1024], BF16))
                    for i in range(2)]
            stwB = [bst.enter_context(nc.sbuf_tensor(f"stB{s}_{i}", [128, 16], F32))
                    for i in range(2)]
            s_ps = [bst.enter_context(nc.psum_tensor(f"sps{s}_{i}", [128, 1024], F32))
                    for i in range(2)]
            ptps = [bst.enter_context(nc.psum_tensor(f"ptps{s}_0", [128, 1024], BF16))]
            o_ps = [bst.enter_context(nc.psum_tensor(f"ops{s}_{i}", [128, 512], F32))
                    for i in range(2)]
            at_ps = bst.enter_context(nc.psum_tensor(f"atps{s}", [128, 512], F32))
            rb_ps = at_ps

            em.op("gpsimd", "dma_start", wo_sb[:], dr["wo"][:], incs=[(T["dWo"], 16)])
            tok_wo = T["dWo"].v

            sps_rd = [0, 0]
            pt_rd = [0, 0]
            ptps_rd = [[]]
            ops_rd = [[], []]
            osb_rd = [0, 0]
            rrow_rd = [0, 0]
            rb_rd = 0
            atps_rd = 0
            xtB_tok = [0, 0]
            xtB_rd = [0, 0]
            x2t_rd = [[], []]
            stB_rd = [0, 0]
            ln3t_rd = [0, 0]
            atsb_rd = [0, 0]
            pending_ln3 = []
            mask_wait = True

            rr = 0
            orot = 0
            def emit_pair_mms(qg, c):
                nonlocal rr, mask_wait
                ch = c
                nkt = 4 * (qg + 1)
                exp_tok = [0] * nkt

                def emit_st(kt):
                    nonlocal rr, mask_wait
                    sl = rr % 2
                    rr += 1
                    em.wait("tensor", sS, sps_rd[sl])
                    for hh in range(2):
                        pb = 64 * hh
                        em.op("tensor", "matmul",
                              s_ps[sl][:, hh * 512:(hh + 1) * 512],
                              k_fm[pb:pb + 64, ch * 2048 + kt * 128: ch * 2048 + (kt + 1) * 128],
                              q_fm[pb:pb + 64, ch * 2048 + qg * 512: ch * 2048 + (qg + 1) * 512],
                              start=True, stop=True, incs=[(tPE, 1)])
                    tok_st = (tPE, tPE.v)
                    r = kt - qg * 4
                    if r >= 0:
                        em.wait("vector", tPE, tok_st[1])
                        if mask_wait:
                            em.wait("vector", dPro, T["tok_prologue"])
                            mask_wait = False
                        for hh in range(2):
                            em.op("vector", "tensor_tensor",
                                  s_ps[sl][:, hh * 512:(hh + 1) * 512],
                                  s_ps[sl][:, hh * 512:(hh + 1) * 512],
                                  T["masks"][:, r * 512:(r + 1) * 512],
                                  ALU.add, incs=[(vV, 1)])
                        tok_st = (vV, vV.v)
                    em.wait("scalar", tok_st[0], tok_st[1])
                    em.wait("scalar", tPE, pt_rd[sl])
                    em.op("scalar", "activation", pt_sb[sl][:], s_ps[sl][:],
                          AF.Exp, bias=EXPB, incs=[(sS, 1)])
                    sps_rd[sl] = sS.v
                    exp_tok[kt] = sS.v
                    return sl

                slots = [emit_st(kk) for kk in range(min(2, nkt))]
                for kt in range(nkt):
                    sl = slots[kt]
                    em.wait("tensor", sS, exp_tok[kt])
                    if kt == 0:
                        em.waits("tensor", ops_rd[0])
                        em.waits("tensor", ops_rd[1])
                        ops_rd[0] = []
                        ops_rd[1] = []
                    for hh in range(2):
                        h = 2 * c + hh
                        em.op("tensor", "matmul", o_ps[hh][0:65, :],
                              v_tm[:, kt * 1040 + h * 65: kt * 1040 + (h + 1) * 65],
                              pt_sb[sl][:, hh * 512:(hh + 1) * 512],
                              start=(kt == 0), stop=(kt == nkt - 1),
                              incs=([(tPE, 1)] if hh == 1 else []))
                    pt_rd[sl] = tPE.v
                    if kt + 2 < nkt:
                        slots.append(emit_st(kt + 2))
                return (c, tPE.v)

            def emit_fin_a(fin):
                c, tok_pv = fin
                toks = []
                for hh in range(2):
                    em.wait("scalar", tPE, tok_pv)
                    em.wait("scalar", vV, osb_rd[hh])
                    em.op("scalar", "activation", o_sb[hh][0:64, 0:512],
                          o_ps[hh][0:64, :], AF.Identity, incs=[(sS, 1)])
                    tok_osb = sS.v
                    em.wait("scalar", sS)
                    em.op("scalar", "activation", lrow32[0:1, :],
                          o_ps[hh][64:65, 0:512], AF.Ln, incs=[(sS, 1)])
                    em.wait("scalar", sS)
                    em.wait("scalar", tPE, rrow_rd[hh])
                    em.op("scalar", "activation", r_row[hh][0:1, :],
                          lrow32[0:1, :], AF.Exp, scale=NEG1_S, incs=[(sS, 1)])
                    tok_rr = sS.v
                    ops_rd[hh] = [(sS, tok_rr)]
                    toks.append((tok_osb, tok_rr))
                return (c, tok_pv, toks)

            def emit_fin_b(fin):
                nonlocal rb_rd
                c, tok_pv, toks = fin
                for hh in range(2):
                    tok_osb, tok_rr = toks[hh]
                    pb = 64 * hh
                    em.wait("tensor", sS, tok_rr)
                    em.wait("tensor", vV, rb_rd)
                    em.wait("tensor", sS, atps_rd)
                    em.op("tensor", "matmul", rb_ps[0:64, :],
                          T["onec"][0:1, 0:64], r_row[hh][0:1, :],
                          start=True, stop=True, incs=[(tPE, 1)])
                    tok_rb = tPE.v
                    rrow_rd[hh] = tok_rb
                    em.wait("vector", tPE, tok_rb)
                    em.wait("vector", sS, tok_osb)
                    em.op("vector", "tensor_tensor",
                          ofm[pb:pb + 64, c * 512:(c + 1) * 512],
                          o_sb[hh][0:64, 0:512], rb_ps[0:64, :], ALU.mult,
                          incs=[(vV, 1)])
                    rb_rd = vV.v
                    osb_rd[hh] = vV.v

            pending_fin = None
            for qg in range(NQG):
                for c in range(H // 2):
                    if pending_fin is not None:
                        pending_fin = emit_fin_a(pending_fin)
                    fin = emit_pair_mms(qg, c)
                    if pending_fin is not None:
                        emit_fin_b(pending_fin)
                    pending_fin = fin
                pending_fin = emit_fin_a(pending_fin)
                emit_fin_b(pending_fin)
                pending_fin = None
                tok_ofm = vV.v
                def emit_ln3_tp(job):
                    qt_, slx_, tok_apply = job
                    em.wait("tensor", sS, tok_apply)
                    em.waits("tensor", ptps_rd[0])
                    ptps_rd[0] = []
                    for c_ in range(8):
                        em.op("tensor", "transpose",
                              ptps[0][:, c_ * 128:(c_ + 1) * 128],
                              ln3t[slx_][:, c_ * 128:(c_ + 1) * 128], ident[:],
                              incs=([(tPE, 1)] if c_ == 7 else []))
                    ln3t_rd[slx_] = tPE.v
                    em.wait("vector", tPE, tPE.v)
                    em.op("vector", "tensor_copy",
                          fmx[:].rearrange("p (c t) -> p c t", c=8)[:, :, qt_ * 128:(qt_ + 1) * 128],
                          ptps[0][:].rearrange("p (c u) -> p c u", c=8),
                          incs=[(vV, 1)])
                    ptps_rd[0].append((vV, vV.v))

                for jt in range(4):
                    qt = qg * 4 + jt
                    slx = jt % 2
                    em.wait("sync", vV, xtB_rd[0])
                    em.op("sync", "dma_start", xtB[0][:],
                          xin[s, qt * 128:(qt + 1) * 128, :], incs=[(dXb, 16)])
                    xtB_tok[0] = dXb.v
                    em.wait("tensor", vV, tok_ofm)
                    em.wait("tensor", T["dWo"], tok_wo)
                    em.wait("tensor", vV, rb_rd)
                    for fo in range(2):
                        em.wait("tensor", sS, atps_rd)
                        for kc in range(8):
                            em.op("tensor", "matmul",
                                  at_ps[:],
                                  ofm[:, kc * 512 + jt * 128: kc * 512 + (jt + 1) * 128],
                                  wo_sb[:, kc * 1024 + fo * 512: kc * 1024 + (fo + 1) * 512],
                                  start=(kc == 0), stop=(kc == 7),
                                  incs=([(tPE, 1)] if kc == 7 else []))
                        em.wait("scalar", tPE, tPE.v)
                        if fo == 0:
                            em.wait("scalar", vV, atsb_rd[slx])
                        em.op("scalar", "activation",
                              atsb[slx][:, fo * 512:(fo + 1) * 512], at_ps[:],
                              AF.Identity, incs=[(sS, 1)])
                        atps_rd = sS.v
                    tok_wmm = tPE.v
                    if len(pending_ln3) >= 2:
                        emit_ln3_tp(pending_ln3.pop(0))
                    tok_atsb = sS.v
                    em.wait("vector", sS, stB_rd[slx])
                    r_ap, nm_ap, tok_nm, tok_r = _ln_stats(
                        em, T, stwB[slx], atsb[slx][:], T["scr"], 0,
                        [(sS, tok_atsb)], [(sS, tok_atsb)])
                    em.wait("vector", dXb, xtB_tok[0])
                    em.wait("vector", vV, tok_nm)
                    em.op("vector", "tensor_scalar", T["scr"][:], xtB[0][:],
                          nm_ap, None, ALU.add, incs=[(vV, 1)])
                    xtB_rd[0] = vV.v
                    em.wait("vector", sS, tok_r)
                    em.waits("vector", x2t_rd[slx])
                    x2t_rd[slx] = []
                    em.wait("vector", vV)
                    em.op("vector", "scalar_tensor_tensor", x2t[slx][:],
                          atsb[slx][:], r_ap, T["scr"][:], ALU.mult, ALU.add,
                          incs=[(vV, 1)])
                    atsb_rd[slx] = vV.v
                    tok_x2 = vV.v
                    em.wait("sync", vV, tok_x2)
                    em.op("sync", "dma_start", x2scr[s, qt * 128:(qt + 1) * 128, :],
                          x2t[slx][:], incs=[(dS[slx], 16)])
                    x2t_rd[slx].append((dS[slx], dS[slx].v))
                    r3_ap, nm3_ap, tok_nm3, tok_r3 = _ln_stats(
                        em, T, stwB[slx], x2t[slx][:], T["scr"], 8,
                        [(vV, tok_x2)], [(vV, tok_x2)])
                    em.wait("scalar", vV, tok_nm3)
                    em.wait("scalar", tPE, ln3t_rd[slx])
                    em.op("scalar", "activation", ln3t[slx][:], x2t[slx][:],
                          AF.Identity, bias=nm3_ap, scale=r3_ap, incs=[(sS, 1)])
                    stB_rd[slx] = sS.v
                    x2t_rd[slx].append((sS, sS.v))
                    pending_ln3.append((qt, slx, sS.v))
            while pending_ln3:
                emit_ln3_tp(pending_ln3.pop(0))
            em.flush()

    if "F" in _PHASES:
        _emit_ffn(nc, st, em, s, T)


def _emit_ffn(nc, st, em, s, T):
    xout, x2scr, dr = T["xout"], T["x2scr"], T["dr"]
    fmx = T["fmx"]
    dPro = T["dPro"]
    dF13 = [T["dF13_0"], T["dF13_1"]]
    dF2 = [T["dF2_0"], T["dF2_1"], T["dF2_2"], T["dF2_3"]]
    dFc = T["dFc"]
    dS = [T["dS0"], T["dS1"]]
    dXf = [T["dXf0"], T["dXf1"]]
    dO = [T["dO0"], T["dO1"]]
    tPE, vV, sS = T["tPE"], T["vV"], T["sS"]
    fm = lambda c, t0, n: fmx[:, c * 2048 + t0: c * 2048 + t0 + n]

    for _e in ("sync", "gpsimd", "tensor", "vector", "scalar"):
        em.wait(_e, T["dS0"])
        em.wait(_e, T["dS1"])
        em.wait(_e, T["dXb"])
    with ExitStack() as fst:
        wbuf = [fst.enter_context(nc.sbuf_tensor(f"wF{s}_{i}", [128, 8192], BF16))
                for i in range(2)]
        w2buf = [fst.enter_context(nc.sbuf_tensor(f"w2F{s}_{i}", [128, 4096], BF16))
                 for i in range(4)]
        hsb = [fst.enter_context(nc.sbuf_tensor(f"hs{s}_{i}", [128, 4096], BF16))
               for i in range(4)]
        ysb = [fst.enter_context(nc.sbuf_tensor(f"ysb{s}_{j}", [128, 1024], F32))
               for j in range(JT_G)]
        h1sb = [fst.enter_context(nc.sbuf_tensor(f"h1sb{s}_{i}", [128, 512], F32))
                for i in range(2)]
        hstb = fst.enter_context(nc.sbuf_tensor(f"hstF{s}", [128, 512], F32))
        gwork = fst.enter_context(nc.sbuf_tensor(f"gwork{s}", [128, 1024], F32))
        comb = fst.enter_context(nc.sbuf_tensor(f"comb{s}", [128, 128], BF16))
        cemb = [fst.enter_context(nc.sbuf_tensor(f"cemb{s}_{i}", [16, FG], BF16))
                for i in range(2)]
        x2tF = [fst.enter_context(nc.sbuf_tensor(f"x2F{s}_0", [128, 1024], F32))]
        outt = [fst.enter_context(nc.sbuf_tensor(f"outt{s}_{i}", [128, 1024], F32))
                for i in range(2)]
        stwF = fst.enter_context(nc.sbuf_tensor(f"stF{s}", [128, 16], F32))
        hps = [fst.enter_context(nc.psum_tensor(f"hps{s}_{i}", [128, 512], F32))
               for i in range(2)]
        cbps = fst.enter_context(nc.psum_tensor(f"cbps{s}", [128, 512], F32))
        yps = [fst.enter_context(nc.psum_tensor(f"yps{s}_{i}", [128, 1024], F32))
               for i in range(2)]
        ctps = fst.enter_context(nc.psum_tensor(f"ctps{s}", [128, 1024], BF16))
        selm = fst.enter_context(nc.sbuf_tensor(f"selmF{s}", [16, 2048], BF16))
        be2s = fst.enter_context(nc.sbuf_tensor(f"be2F{s}", [16, 1024], BF16))
        bs2s = fst.enter_context(nc.sbuf_tensor(f"bs2F{s}", [1, 1024], BF16))
        em.op("gpsimd", "dma_start", selm[:], dr["csel"][:], incs=[(dFc, 16)])
        em.op("gpsimd", "dma_start", be2s[:], dr["be2"][:], incs=[(dFc, 16)])
        em.op("gpsimd", "dma_start", bs2s[:], dr["bs2"][:], incs=[(dFc, 16)])
        tok_fconst = dFc.v

        wbuf_rd = [0, 0]
        w2_rd = [0] * 4
        hs_rd = [0] * 4
        hps_rd = [[], []]
        h1sb_rd = [0, 0]
        h1rot = [0]
        cb_rd = 0
        yps_rd = [0, 0]
        ysb_rd = [0] * JT_G
        cemb_rd = [0, 0]
        comb_rd = 0
        ctps_rd = 0
        x2F_rd = [0, 0]
        out_rd = [0, 0]
        x2_stores = (dS[0].v, dS[1].v)

        def emit_gate(g):
            nonlocal cb_rd, comb_rd, ctps_rd
            GW = gwork
            for jt in range(JT_G):
                qt = g * JT_G + jt
                em.wait("tensor", vV, cb_rd)
                if s == 0 and g == 0 and jt == 0:
                    em.wait("tensor", dPro, T["tok_prologue"])
                for kc in range(8):
                    em.op("tensor", "matmul", cbps[:, jt * 16:(jt + 1) * 16],
                          fm(kc, qt * 128, 128), T["gwt"][:, kc * 16:(kc + 1) * 16],
                          start=(kc == 0), stop=(kc == 7),
                          incs=([(tPE, 1)] if kc == 7 else []))
            tok_gmm = tPE.v
            em.wait("vector", tPE, tok_gmm)
            em.op("vector", "tensor_copy", GW[:, 0:128], cbps[:, 0:128],
                  incs=[(vV, 1)])
            cb_rd = vV.v
            em.wait("scalar", vV, vV.v)
            em.op("scalar", "activation", GW[:, 0:128], GW[:, 0:128], AF.Exp,
                  incs=[(sS, 1)])
            gv = lambda off: GW[:, off:off + 128].rearrange("p (j i) -> p j i", i=16)
            bc8 = lambda off: bass.AP(gwork, off, [[1024, 128], [1, 8], [0, 16]])
            bc8g = lambda off: bass.AP(gwork, off, [[1024, 128], [1, 8], [0, 4]])
            em.wait("vector", sS, sS.v)
            em.op("vector", "tensor_reduce", GW[:, 608:616],
                  gv(0), AX.X, ALU.add, incs=[(vV, 1)])
            em.wait("vector", vV)
            em.op("vector", "reciprocal", GW[:, 616:624], GW[:, 608:616],
                  incs=[(vV, 1)])
            em.wait("vector", vV)
            em.op("vector", "tensor_tensor", gv(128), gv(0), bc8(616),
                  ALU.mult, incs=[(vV, 1)])
            em.wait("vector", vV)
            em.op("vector", "tensor_tensor", gv(256), gv(128),
                  bass.AP(T["gb16"], 0, [[16, 128], [0, 8], [1, 16]]),
                  ALU.add, incs=[(vV, 1)])
            em.wait("vector", vV)
            pv = lambda i: bass.AP(gwork, 256 + i, [[1024, 128], [16, 8], [4, 4]])
            pi = 0
            for i in range(4):
                for j in range(i + 1, 4):
                    em.op("vector", "tensor_tensor",
                          GW[:, 384 + pi * 32:384 + pi * 32 + 32].rearrange(
                              "p (j i) -> p j i", i=4),
                          pv(i), pv(j), ALU.add, incs=[(vV, 1)])
                    pi += 1
            em.wait("vector", vV)
            em.op("vector", "tensor_tensor", GW[:, 576:608], GW[:, 384:416],
                  GW[:, 416:448], ALU.max, incs=[(vV, 1)])
            for pi in range(2, 6):
                em.wait("vector", vV)
                em.op("vector", "tensor_tensor", GW[:, 576:608], GW[:, 576:608],
                      GW[:, 384 + pi * 32:416 + pi * 32], ALU.max, incs=[(vV, 1)])
            gsv = GW[:, 576:608].rearrange("p (j i) -> p j i", i=4)
            em.wait("vector", vV)
            em.op("vector", "tensor_reduce", GW[:, 624:632], gsv, AX.X, ALU.max,
                  incs=[(vV, 1)])
            em.wait("vector", vV)
            em.op("vector", "tensor_tensor", GW[:, 640:672].rearrange(
                      "p (j i) -> p j i", i=4), gsv, bc8g(624),
                  ALU.is_ge, incs=[(vV, 1)])
            em.wait("vector", vV)
            em.op("vector", "scalar_tensor_tensor", GW[:, 672:704],
                  GW[:, 640:672], NBIG, GW[:, 576:608], ALU.mult, ALU.add,
                  incs=[(vV, 1)])
            em.wait("vector", vV)
            em.op("vector", "tensor_reduce", GW[:, 632:640],
                  GW[:, 672:704].rearrange("p (j i) -> p j i", i=4), AX.X,
                  ALU.max, incs=[(vV, 1)])
            em.wait("vector", vV)
            em.op("vector", "tensor_tensor", GW[:, 640:672].rearrange(
                      "p (j i) -> p j i", i=4), gsv, bc8g(632),
                  ALU.is_ge, incs=[(vV, 1)])
            em.wait("vector", vV)
            em.op("vector", "tensor_copy",
                  GW[:, 704:832].rearrange("p (j g i) -> p j g i", g=4, i=4),
                  bass.AP(gwork, 640, [[1024, 128], [4, 8], [1, 4], [0, 4]]),
                  incs=[(vV, 1)])
            em.wait("vector", vV)
            em.op("vector", "tensor_scalar", GW[:, 704:832], GW[:, 704:832],
                  1.0, BIG, ALU.subtract, ALU.mult, incs=[(vV, 1)])
            em.wait("vector", vV)
            em.op("vector", "tensor_tensor", GW[:, 384:512], GW[:, 256:384],
                  GW[:, 704:832], ALU.add, incs=[(vV, 1)])
            em.wait("vector", vV)
            em.op("vector", "tensor_reduce", GW[:, 624:632], gv(384), AX.X,
                  ALU.max, incs=[(vV, 1)])
            em.wait("vector", vV)
            em.op("vector", "tensor_tensor", gv(512), gv(384), bc8(624),
                  ALU.is_ge, incs=[(vV, 1)])
            em.wait("vector", vV)
            em.op("vector", "scalar_tensor_tensor", GW[:, 640:768],
                  GW[:, 512:640], NBIG, GW[:, 384:512], ALU.mult, ALU.add,
                  incs=[(vV, 1)])
            em.wait("vector", vV)
            em.op("vector", "tensor_reduce", GW[:, 632:640], gv(640), AX.X,
                  ALU.max, incs=[(vV, 1)])
            em.wait("vector", vV)
            em.op("vector", "tensor_tensor", gv(512), gv(384), bc8(632),
                  ALU.is_ge, incs=[(vV, 1)])
            em.wait("vector", vV)
            em.wait("vector", tPE, comb_rd)
            em.op("vector", "tensor_tensor", comb[:], GW[:, 128:256],
                  GW[:, 512:640], ALU.mult, incs=[(vV, 1)])
            tok_comb = vV.v
            for jt in range(JT_G):
                em.wait("tensor", vV, tok_comb)
                em.wait("tensor", vV, ctps_rd)
                em.op("tensor", "transpose", ctps[0:16, jt * 128:(jt + 1) * 128],
                      comb[:, jt * 16:(jt + 1) * 16], T["ident"][:],
                      incs=[(tPE, 1)])
            comb_rd = tPE.v
            em.wait("vector", tPE, tPE.v)
            em.wait("vector", tPE, cemb_rd[g])
            em.op("vector", "tensor_copy", cemb[g][:],
                  ctps[0:16, 0:1024], incs=[(vV, 1)])
            ctps_rd = vV.v
            cemb_tok[g] = vV.v

        cemb_tok = [0, 0]
        emit_gate(0)
        emit_gate(1)

        for g in range(NFG):


            for ci, chunk in enumerate(CHUNKS):
                for e in chunk:
                    wsl = e % 2
                    w2sl = (e % 4) if e < 16 else (e - 16)
                    em.wait("gpsimd", tPE, wbuf_rd[wsl])
                    em.op("gpsimd", "dma_start", wbuf[wsl][:], dr["we13"][e, :, :],
                          incs=[(dF13[wsl], 16)])
                    tok_w13 = (dF13[wsl], dF13[wsl].v)
                    em.wait("gpsimd", tPE, w2_rd[w2sl])
                    em.op("gpsimd", "dma_start", w2buf[w2sl][:], dr["we2"][e, :, :],
                          incs=[(dF2[w2sl], 16)])
                    for tch in range(FG // 512):
                        if e < 16:
                            em.wait("tensor", vV, cb_rd)
                            em.wait("tensor", vV, cemb_tok[g])
                            em.wait("tensor", dFc, tok_fconst)
                            em.op("tensor", "matmul", cbps[:],
                                  selm[:, e * 128:(e + 1) * 128],
                                  cemb[g][:, tch * 512:(tch + 1) * 512],
                                  start=True, stop=True, incs=[(tPE, 1)])
                            tok_cb = tPE.v
                        for ic in range(4):
                            s1, s3 = 0, 1
                            em.wait("tensor", tok_w13[0], tok_w13[1])
                            em.waits("tensor", hps_rd[s1])
                            hps_rd[s1] = []
                            for kc in range(8):
                                em.op("tensor", "matmul", hps[s1][:],
                                      wbuf[wsl][:, kc * 1024 + ic * 128: kc * 1024 + ic * 128 + 128],
                                      fm(kc, g * FG + tch * 512, 512),
                                      start=(kc == 0), stop=(kc == 7),
                                      incs=([(tPE, 1)] if kc == 7 else []))
                            tok_h1 = tPE.v
                            em.waits("tensor", hps_rd[s3])
                            hps_rd[s3] = []
                            for kc in range(8):
                                em.op("tensor", "matmul", hps[s3][:],
                                      wbuf[wsl][:, kc * 1024 + 512 + ic * 128: kc * 1024 + 512 + ic * 128 + 128],
                                      fm(kc, g * FG + tch * 512, 512),
                                      start=(kc == 0), stop=(kc == 7),
                                      incs=([(tPE, 1)] if kc == 7 else []))
                            tok_h3 = tPE.v
                            wbuf_rd[wsl] = tok_h3
                            hsl = h1rot[0] % 2
                            h1rot[0] += 1
                            em.wait("scalar", tPE, tok_h1)
                            em.wait("scalar", vV, h1sb_rd[hsl])
                            em.op("scalar", "activation", h1sb[hsl][:], hps[s1][:],
                                  AF.Silu,
                                  bias=T["bia1"][:, e * 4 + ic:e * 4 + ic + 1],
                                  incs=[(sS, 1)])
                            hps_rd[s1].append((sS, sS.v))
                            em.wait("vector", tPE, tok_h3)
                            em.wait("vector", sS, sS.v)
                            em.wait("vector", vV)
                            em.op("vector", "scalar_tensor_tensor", hstb[:],
                                  hps[s3][:],
                                  T["bia3"][:, e * 4 + ic:e * 4 + ic + 1],
                                  h1sb[hsl][:], ALU.add, ALU.mult, incs=[(vV, 1)])
                            hps_rd[s3].append((vV, vV.v))
                            h1sb_rd[hsl] = vV.v
                            em.wait("vector", vV)
                            em.wait("vector", tPE, hs_rd[w2sl])
                            dsth = hsb[w2sl][:, ic * 1024 + tch * 512: ic * 1024 + (tch + 1) * 512]
                            if e < 16:
                                em.wait("vector", tPE, tok_cb)
                                em.op("vector", "tensor_tensor", dsth, hstb[:],
                                      cbps[:], ALU.mult, incs=[(vV, 1)])
                                cb_rd = vV.v
                            else:
                                em.op("vector", "tensor_copy", dsth, hstb[:],
                                      incs=[(vV, 1)])
                tok_hs_all = vV.v
                tok_w2_all = [(d, d.v) for d in dF2]
                nch = len(chunk)
                for jt in range(JT_G):
                    ysl = jt % 2
                    em.wait("tensor", vV, tok_hs_all)
                    em.waits("tensor", tok_w2_all)
                    em.wait("tensor", vV, yps_rd[ysl])
                    if ci == 0:
                        em.wait("tensor", vV, cemb_tok[g])
                        em.wait("tensor", dFc, tok_fconst)
                        for fo in range(2):
                            em.op("tensor", "matmul",
                                  yps[ysl][:, fo * 512:(fo + 1) * 512],
                                  cemb[g][:, jt * 128:(jt + 1) * 128],
                                  be2s[:, fo * 512:(fo + 1) * 512],
                                  start=True, stop=False)
                            em.op("tensor", "matmul",
                                  yps[ysl][:, fo * 512:(fo + 1) * 512],
                                  T["onec"][:],
                                  bs2s[:, fo * 512:(fo + 1) * 512],
                                  start=False, stop=False)
                    for i4, e in enumerate(chunk):
                        w2sl = (e % 4) if e < 16 else (e - 16)
                        for ic in range(4):
                            st_flag = (ci != 0 and i4 == 0 and ic == 0)
                            for fo in range(2):
                                em.op("tensor", "matmul",
                                      yps[ysl][:, fo * 512:(fo + 1) * 512],
                                      hsb[w2sl][:, ic * 1024 + jt * 128: ic * 1024 + jt * 128 + 128],
                                      w2buf[w2sl][:, ic * 1024 + fo * 512: ic * 1024 + (fo + 1) * 512],
                                      start=st_flag,
                                      stop=(i4 == nch - 1 and ic == 3),
                                      incs=([(tPE, 1)] if (i4 == nch - 1 and ic == 3 and fo == 1) else []))
                    tok_y = tPE.v
                    cemb_rd[g] = tok_y
                    for e in chunk:
                        w2sl = (e % 4) if e < 16 else (e - 16)
                        w2_rd[w2sl] = tok_y
                        hs_rd[w2sl] = tok_y
                    em.wait("vector", tPE, tok_y)
                    em.wait("vector", sS, ysb_rd[jt])
                    if ci == 0:
                        em.op("vector", "tensor_copy", ysb[jt][:], yps[ysl][:],
                              incs=[(vV, 1)])
                    else:
                        em.op("vector", "tensor_tensor", ysb[jt][:], ysb[jt][:],
                              yps[ysl][:], ALU.add, incs=[(vV, 1)])
                    yps_rd[ysl] = vV.v

            for jt in range(JT_G):
                qt = g * JT_G + jt
                slx = jt % 2
                em.wait("sync", T["dS0"], x2_stores[0])
                em.wait("sync", T["dS1"], x2_stores[1])
                em.wait("sync", vV, x2F_rd[0])
                em.op("sync", "dma_start", x2tF[0][:],
                      x2scr[s, qt * 128:(qt + 1) * 128, :], incs=[(dXf[0], 16)])
                tok_x2l = dXf[0].v
                r_ap, nm_ap, tok_nm, tok_r = _ln_stats(
                    em, T, stwF, ysb[jt][:], T["scr"], 0,
                    [(vV, vV.v)], [(vV, vV.v)])
                ysb_rd[jt] = sS.v
                em.wait("vector", dXf[0], tok_x2l)
                em.wait("vector", vV, tok_nm)
                em.op("vector", "tensor_scalar", T["scr"][:], x2tF[0][:],
                      nm_ap, None, ALU.add, incs=[(vV, 1)])
                x2F_rd[0] = vV.v
                em.wait("vector", sS, tok_r)
                em.wait("vector", dO[slx], out_rd[slx])
                em.wait("vector", vV)
                em.op("vector", "scalar_tensor_tensor", outt[slx][:], ysb[jt][:],
                      r_ap, T["scr"][:], ALU.mult, ALU.add, incs=[(vV, 1)])
                tok_out = vV.v
                em.wait("sync", vV, tok_out)
                em.op("sync", "dma_start", xout[s, qt * 128:(qt + 1) * 128, :],
                      outt[slx][:], incs=[(dO[slx], 16)])
                out_rd[slx] = dO[slx].v
        em.flush()


_NC_CACHE = None


def _get_nc():
    global _NC_CACHE
    if _NC_CACHE is None:
        _NC_CACHE = build_program()
    return _NC_CACHE


def _layout_weights(inputs):
    f = np.float32
    out = {}
    chunk_lhs = lambda w: np.ascontiguousarray(
        w.reshape(8, 128, -1).transpose(1, 0, 2).reshape(128, -1))
    Wkv = np.asarray(inputs["L0_attn_Wkv"], f)
    out["wq"] = chunk_lhs(np.asarray(inputs["L0_attn_Wq"], f))
    out["wk"] = chunk_lhs(np.ascontiguousarray(Wkv[:, :D]))
    out["wv"] = chunk_lhs(np.ascontiguousarray(Wkv[:, D:]))
    out["wo"] = chunk_lhs(np.asarray(inputs["L0_attn_Wo"], f))
    gw = np.asarray(inputs["L0_ffn_gate_w"], f)
    out["gwt"] = chunk_lhs(np.ascontiguousarray(gw.T))
    out["gb16"] = np.ascontiguousarray(
        np.broadcast_to(np.asarray(inputs["L0_ffn_gate_b"], f)[None, :], (128, 16)))
    We1 = np.asarray(inputs["L0_ffn_We1"], f)
    We3 = np.asarray(inputs["L0_ffn_We3"], f)
    We2 = np.asarray(inputs["L0_ffn_We2"], f)
    ws1 = np.asarray(inputs["L0_ffn_ws1"], f)
    ws3 = np.asarray(inputs["L0_ffn_ws3"], f)
    ws2 = np.asarray(inputs["L0_ffn_ws2"], f)
    w13 = np.empty((NE, 128, 8192), f)
    w2 = np.empty((NE, 128, 4096), f)
    for e in range(NE):
        if e < 16:
            a, b, c = We1[e], We3[e], We2[e]
        else:
            hf = e - 16
            a = ws1[:, hf * 512:(hf + 1) * 512]
            b = ws3[:, hf * 512:(hf + 1) * 512]
            c = ws2[hf * 512:(hf + 1) * 512, :]
        ar = np.ascontiguousarray(a).reshape(8, 128, 512)
        br = np.ascontiguousarray(b).reshape(8, 128, 512)
        w13[e] = np.concatenate([ar, br], axis=2).transpose(1, 0, 2).reshape(128, 8192)
        w2[e] = np.ascontiguousarray(c).reshape(4, 128, 1024).transpose(1, 0, 2).reshape(128, 4096)
    out["we13"] = w13
    out["we2"] = w2
    lay_b = lambda be, bsh: np.concatenate(
        [be.reshape(16, 4, 128).transpose(2, 0, 1).reshape(128, 64),
         bsh.reshape(2, 4, 128).transpose(2, 0, 1).reshape(128, 8)], axis=1)
    out["bia1"] = np.ascontiguousarray(
        lay_b(np.asarray(inputs["L0_ffn_be1"], f), np.asarray(inputs["L0_ffn_bs1"], f)))
    out["bia3"] = np.ascontiguousarray(
        lay_b(np.asarray(inputs["L0_ffn_be3"], f), np.asarray(inputs["L0_ffn_bs3"], f)))
    out["be2"] = np.asarray(inputs["L0_ffn_be2"], f)
    out["bs2"] = np.asarray(inputs["L0_ffn_bs2"], f).reshape(1, 1024)
    out["cident"] = np.eye(128, dtype=f)
    kl = np.arange(128)[:, None]
    ql = np.arange(512)[None, :]
    out["cmask"] = np.concatenate(
        [np.where(ql >= r * 128 + kl, 0.0, -50000.0).astype(f) for r in range(4)],
        axis=1)
    sel = np.zeros((16, 2048), f)
    for e in range(16):
        sel[e, e * 128:(e + 1) * 128] = 1.0
    out["csel"] = sel
    return out


def kernel(**inputs):
    nc = _get_nc()
    w = _layout_weights(inputs)
    x = np.asarray(inputs["L0_attn_x"], np.float32)
    in_maps = []
    for c in range(8):
        m = dict(w)
        m["xin"] = np.ascontiguousarray(x[c * NSEQ:(c + 1) * NSEQ])
        in_maps.append(m)
    res = run_bass_kernel_spmd(nc, in_maps, core_ids=list(range(8)))
    return np.concatenate([np.asarray(r["xout"], np.float32) for r in res.results],
                          axis=0)


if __name__ == "__main__":
    import time
    t0 = time.time()
    nc = build_program()
    n = sum(1 for _ in nc.all_instructions())
    print(f"built in {time.time()-t0:.1f}s, {n} instructions")


# version 18
# speedup vs baseline: 1.2449x; 1.0782x over previous
import numpy as np
import ml_dtypes
from contextlib import ExitStack

import concourse.bass as bass
import concourse.mybir as mybir
from concourse.bass_utils import run_bass_kernel_spmd

F32 = mybir.dt.float32
BF16 = mybir.dt.bfloat16
AF = mybir.ActivationFunctionType
ALU = mybir.AluOpType
AX = mybir.AxisListType

NSEQ = 4
S = 2048
D = 1024
H = 16
HD = 64
NT = S // 128
NQG = 4
NE = 18
FG = 1024
NFG = S // FG
JT_G = FG // 128
INV_D = 1.0 / 1024.0
EPS = np.float32(1e-5)
NEG1 = -1.0
NEG1_S = np.float32(-1.0)
NEGH = np.float32(-0.5)
QSCALE = np.float32(0.125)
EXPB = np.float32(-30.0)
BIG = 1e9
NBIG = -1e9

CHUNKS = [(0, 1, 2, 3), (4, 5, 6, 7), (8, 9, 10, 11), (12, 13, 14, 15), (16, 17)]

_PHASES = "ABF"
_NSEQ_RUN = NSEQ
_B_LIMIT = "full"


class Sem:
    def __init__(self, stack, nc, name):
        self.h = stack.enter_context(nc.semaphore(name))
        self.v = 0


class Emitter:

    def __init__(self, nc):
        self.nc = nc
        self.ops = []
        self.waited = {}

    def op(self, eng, method, *args, incs=(), **kwargs):
        self.ops.append((eng, method, args, kwargs, list(incs)))
        for sem, k in incs:
            sem.v += k

    def wait(self, eng, sem, val=None):
        v = sem.v if val is None else val
        key = (eng, sem.h.num)
        if v > 0 and v > self.waited.get(key, 0):
            self.waited[key] = v
            self.ops.append((eng, "wait_ge", (sem.h, v), {}, []))

    def waits(self, eng, deps):
        for sem, tok in deps:
            self.wait(eng, sem, tok)

    def flush(self):
        ops = self.ops
        self.ops = []
        if not ops:
            return
        with self.nc.Block() as blk:
            for name in ("sync", "gpsimd", "tensor", "vector", "scalar"):
                mine = [o for o in ops if o[0] == name]
                if not mine:
                    continue

                def body(eng, mine=mine):
                    for (_, method, args, kwargs, incs) in mine:
                        if method == "wait_ge":
                            eng.wait_ge(*args)
                            continue
                        bi = getattr(eng, method)(*args, **kwargs)
                        for sem, k in incs:
                            bi.then_inc(sem.h, k)

                getattr(blk, name)(body)


def _ln_stats(em, T, stw, src_ap, sqd, col0, deps_v, deps_s):
    vV, sS = T["vV"], T["sS"]
    c = lambda i: stw[:, col0 + i:col0 + i + 1]
    em.waits("vector", deps_v)
    em.waits("scalar", deps_s)
    em.op("vector", "reduce_sum", c(0), src_ap, axis=AX.X, incs=[(vV, 1)])
    em.wait("scalar", vV, vV.v)
    em.op("scalar", "activation", sqd[:], src_ap, AF.Square,
          accum_out=c(1), incs=[(sS, 1)])
    tok_sq = sS.v
    em.wait("vector", vV)
    em.op("vector", "tensor_scalar", c(2), c(0), INV_D, None, ALU.mult, incs=[(vV, 1)])
    em.wait("vector", sS, tok_sq)
    em.op("vector", "tensor_scalar", c(3), c(1), INV_D, None, ALU.mult, incs=[(vV, 1)])
    em.wait("vector", vV)
    em.op("vector", "tensor_scalar", c(4), c(2), c(2), None, ALU.mult, incs=[(vV, 1)])
    em.wait("vector", vV)
    em.op("vector", "tensor_tensor", c(5), c(3), c(4), ALU.subtract, incs=[(vV, 1)])
    em.wait("scalar", vV, vV.v)
    em.op("scalar", "activation", c(6), c(5), AF.Ln, bias=EPS, incs=[(sS, 1)])
    em.wait("scalar", sS)
    em.op("scalar", "activation", c(5), c(6), AF.Exp, scale=NEGH, incs=[(sS, 1)])
    tok_r = sS.v
    em.wait("vector", sS, tok_r)
    em.op("vector", "tensor_scalar", c(6), c(2), c(5), NEG1, ALU.mult, ALU.mult,
          incs=[(vV, 1)])
    tok_nm = vV.v
    return c(5), c(6), tok_nm, tok_r


def build_program():
    nc = bass.Bass("TRN2", target_bir_lowering=False)

    xin = nc.dram_tensor("xin", [NSEQ, S, D], F32, kind="ExternalInput").ap()
    dr = {}
    for nm, shp in [("wq", [128, 8192]), ("wk", [128, 8192]), ("wv", [128, 8192]),
                    ("wo", [128, 8192]), ("gwt", [128, 128]), ("gb16", [128, 16]),
                    ("bia1", [128, 72]), ("bia3", [128, 72]), ("be2", [16, 1024]),
                    ("bs2", [1, 1024]), ("we13", [NE, 128, 8192]),
                    ("we2", [NE, 128, 4096]), ("cident", [128, 128]),
                    ("cmask", [128, 2048]), ("csel", [16, 2048])]:
        dr[nm] = nc.dram_tensor(nm, shp, F32, kind="ExternalInput").ap()
    xout = nc.dram_tensor("xout", [NSEQ, S, D], F32, kind="ExternalOutput").ap()
    x2scr = nc.dram_tensor("x2scr", [NSEQ, S, D], F32, kind="Internal").ap()

    st = ExitStack()
    with st:
        em = Emitter(nc)
        T = {"nc": nc, "em": em, "xin": xin, "xout": xout, "x2scr": x2scr, "dr": dr}
        for nm in ("dPro", "dWa0", "dWa1", "dWo", "dXa0", "dXa1", "dXb",
                   "dF13_0", "dF13_1", "dF2_0", "dF2_1", "dF2_2", "dF2_3",
                   "dFc", "dS0", "dS1", "dXf0", "dXf1", "dO0", "dO1",
                   "tPE", "vV", "sS"):
            T[nm] = Sem(st, nc, nm)
        dPro = T["dPro"]

        sb = lambda name, shp, dt=BF16: st.enter_context(nc.sbuf_tensor(name, shp, dt))
        T["ident"] = sb("ident", [128, 128])
        T["masks"] = sb("masks", [128, 2048])
        T["gwt"] = sb("gwtS", [128, 128])
        T["gb16"] = sb("gb16S", [128, 16], F32)
        T["bia1"] = sb("bia1S", [128, 72], F32)
        T["bia3"] = sb("bia3S", [128, 72], F32)
        T["onec"] = sb("onec", [1, 128])
        T["fmx"] = sb("fmx", [128, 8 * 2048])
        T["scr"] = sb("scr", [128, 1024], F32)
        T["ones1f"] = sb("ones1f", [1, 64], F32)

        for dst, srcnm in [(T["ident"], "cident"), (T["masks"], "cmask"),
                           (T["gwt"], "gwt"),
                           (T["gb16"], "gb16"), (T["bia1"], "bia1"),
                           (T["bia3"], "bia3")]:
            em.op("gpsimd", "dma_start", dst[:], dr[srcnm][:], incs=[(dPro, 16)])
        em.op("vector", "memset", T["onec"][:], 1.0)
        em.op("vector", "memset", T["ones1f"][:], 1.0, incs=[(T["vV"], 1)])
        T["tok_prologue"] = dPro.v
        em.flush()

        for s in range(_NSEQ_RUN):
            _emit_seq(nc, st, em, s, T)

        em.wait("sync", T["dO0"])
        em.wait("sync", T["dO1"])
        em.flush()

    nc.finalize()
    return nc


def _emit_seq(nc, st, em, s, T):
    xin, xout, x2scr, dr = T["xin"], T["xout"], T["x2scr"], T["dr"]
    fmx, ident = T["fmx"], T["ident"]
    dPro = T["dPro"]
    dWa = [T["dWa0"], T["dWa1"]]
    dXa = [T["dXa0"], T["dXa1"]]
    dXb = T["dXb"]
    dS = [T["dS0"], T["dS1"]]
    tPE, vV, sS = T["tPE"], T["vV"], T["sS"]
    fm = lambda c, t0, n: fmx[:, c * 2048 + t0: c * 2048 + t0 + n]

    if "A" not in _PHASES:
        return
    for _e in ("sync", "gpsimd", "tensor", "vector", "scalar"):
        em.wait(_e, T["dO0"])
        em.wait(_e, T["dO1"])
        em.wait(_e, T["dXf0"])
        em.wait(_e, T["dXf1"])
    with ExitStack() as qst:
        q_fm = qst.enter_context(nc.sbuf_tensor(f"q_fm{s}", [128, 8 * 2048], BF16))
        k_fm = qst.enter_context(nc.sbuf_tensor(f"k_fm{s}", [128, 8 * 2048], BF16))
        v_tm = qst.enter_context(nc.sbuf_tensor(f"v_tm{s}", [128, 16 * 1040], BF16))

        with ExitStack() as ast:
            xt = [ast.enter_context(nc.sbuf_tensor(f"xtA{s}_{i}", [128, 1024], F32))
                  for i in range(2)]
            ln1t = [ast.enter_context(nc.sbuf_tensor(f"ln1t{s}_{i}", [128, 1024], BF16))
                    for i in range(2)]
            stw = [ast.enter_context(nc.sbuf_tensor(f"stA{s}_{i}", [128, 16], F32))
                   for i in range(2)]
            wbufA = [ast.enter_context(nc.sbuf_tensor(f"wA{s}_{i}", [128, 8192], BF16))
                     for i in range(2)]
            ps = [ast.enter_context(nc.psum_tensor(f"psA{s}_{i}", [128, 512], F32))
                  for i in range(2)]
            tp = [ast.enter_context(nc.psum_tensor(f"tpA{s}_{i}", [128, 1024], BF16))
                  for i in range(2)]

            em.op("gpsimd", "dma_start", wbufA[0][:], dr["wk"][:], incs=[(dWa[0], 16)])
            tok_wk = (dWa[0], dWa[0].v)
            em.op("gpsimd", "dma_start", wbufA[1][:], dr["wv"][:], incs=[(dWa[1], 16)])
            tok_wv = (dWa[1], dWa[1].v)

            xt_vrd = [0, 0]
            xt_srd = [0, 0]
            tp_rd = [0, 0]
            ln1t_rd = [0, 0]
            st_rd = [0, 0]
            ln1_tok = [0] * NT

            ps_rd = [0, 0]
            ridx = [0]

            def emit_tile_ln(qt):
                sl = qt % 2
                em.wait("sync", vV, xt_vrd[sl])
                em.wait("sync", sS, xt_srd[sl])
                em.op("sync", "dma_start", xt[sl][:],
                      xin[s, qt * 128:(qt + 1) * 128, :], incs=[(dXa[sl], 16)])
                tok_x = dXa[sl].v
                em.wait("vector", sS, st_rd[sl])
                r_ap, nm_ap, tok_nm, tok_r = _ln_stats(
                    em, T, stw[sl], xt[sl][:], T["scr"], 0,
                    [(dXa[sl], tok_x)], [(dXa[sl], tok_x)])
                em.wait("scalar", tPE, ln1t_rd[sl])
                em.wait("scalar", vV, tok_nm)
                em.op("scalar", "activation", ln1t[sl][:], xt[sl][:], AF.Identity,
                      bias=nm_ap, scale=r_ap, incs=[(sS, 1)])
                st_rd[sl] = sS.v
                xt_srd[sl] = sS.v
                xt_vrd[sl] = tok_nm
                em.wait("tensor", sS, sS.v)
                em.wait("tensor", vV, tp_rd[sl])
                if qt == 0 and s == 0:
                    em.wait("tensor", dPro, T["tok_prologue"])
                for c_ in range(8):
                    em.op("tensor", "transpose", tp[sl][:, c_ * 128:(c_ + 1) * 128],
                          ln1t[sl][:, c_ * 128:(c_ + 1) * 128], ident[:],
                          incs=([(tPE, 1)] if c_ == 7 else []))
                ln1t_rd[sl] = tPE.v
                em.wait("vector", tPE, tPE.v)
                em.op("vector", "tensor_copy",
                      fmx[:].rearrange("p (c t) -> p c t", c=8)[:, :, qt * 128:(qt + 1) * 128],
                      tp[sl][:].rearrange("p (c u) -> p c u", c=8),
                      incs=[(vV, 1)])
                tp_rd[sl] = vV.v
                ln1_tok[qt] = vV.v

            def proj_tch(dst, wsb, tok_w, scale, tch, fo_list):
                em.wait("tensor", tok_w[0], tok_w[1])
                em.wait("tensor", vV, ln1_tok[tch * 4 + 3])
                for fo in fo_list:
                    sl = ridx[0] % 2
                    ridx[0] += 1
                    em.wait("tensor", sS, ps_rd[sl])
                    for kc in range(8):
                        em.op("tensor", "matmul", ps[sl][:],
                              wsb[:, kc * 1024 + fo * 128: kc * 1024 + fo * 128 + 128],
                              fm(kc, tch * 512, 512),
                              start=(kc == 0), stop=(kc == 7),
                              incs=([(tPE, 1)] if kc == 7 else []))
                    em.wait("scalar", tPE, tPE.v)
                    em.op("scalar", "activation",
                          dst[:, fo * 2048 + tch * 512: fo * 2048 + (tch + 1) * 512],
                          ps[sl][:], AF.Identity, scale=scale, incs=[(sS, 1)])
                    ps_rd[sl] = sS.v
                return tPE.v

            def proj_fm(dst, wsb, tok_w, scale):
                for fo in range(8):
                    for tch in range(4):
                        proj_tch(dst, wsb, tok_w, scale, tch, [fo])
                return tPE.v

            for tch in range(4):
                for qt in range(tch * 4, tch * 4 + 4):
                    emit_tile_ln(qt)
                proj_tch(k_fm, wbufA[0], tok_wk, np.float32(1.0), tch,
                         list(range(8)))
            tok_kmm = tPE.v
            em.wait("gpsimd", tPE, tok_kmm)
            em.op("gpsimd", "dma_start", wbufA[0][:], dr["wq"][:], incs=[(dWa[0], 16)])
            tok_wq = (dWa[0], dWa[0].v)
            em.op("vector", "memset",
                  v_tm[:].rearrange("p (a x) -> p a x", x=65)[:, :, 64:65], 1.0,
                  incs=[(vV, 1)])
            tok_vones = vV.v
            em.wait("tensor", tok_wv[0], tok_wv[1])
            em.wait("tensor", vV, tok_vones)
            for qt in range(NT):
                em.wait("tensor", vV, ln1_tok[qt])
                for fo in range(2):
                    sl = ridx[0] % 2
                    ridx[0] += 1
                    em.wait("tensor", sS, ps_rd[sl])
                    for kc in range(8):
                        em.op("tensor", "matmul", ps[sl][:],
                              fm(kc, qt * 128, 128),
                              wbufA[1][:, kc * 1024 + fo * 512: kc * 1024 + (fo + 1) * 512],
                              start=(kc == 0), stop=(kc == 7),
                              incs=([(tPE, 1)] if kc == 7 else []))
                    em.wait("scalar", tPE, tPE.v)
                    dstv = v_tm[:, qt * 1040 + fo * 520: qt * 1040 + (fo + 1) * 520]
                    em.op("scalar", "activation",
                          dstv.rearrange("p (h x) -> p h x", x=65)[:, :, 0:64],
                          ps[sl][:].rearrange("p (h x) -> p h x", x=64),
                          AF.Identity, incs=[(sS, 1)])
                    ps_rd[sl] = sS.v
            proj_fm(q_fm, wbufA[0], tok_wq, QSCALE)
            em.flush()

        if "B" not in _PHASES:
            em.flush()
            return
        for _e in ("sync", "gpsimd", "tensor", "vector", "scalar"):
            em.wait(_e, T["dXa0"])
            em.wait(_e, T["dXa1"])
            em.wait(_e, T["dWa0"])
            em.wait(_e, T["dWa1"])
        with ExitStack() as bst:
            wo_sb = bst.enter_context(nc.sbuf_tensor(f"wo_sb{s}", [128, 8192], BF16))
            pt_sb = [bst.enter_context(nc.sbuf_tensor(f"ptsb{s}_{i}", [128, 1024], BF16))
                     for i in range(2)]
            o_sb = [bst.enter_context(nc.sbuf_tensor(f"osb{s}_{i}", [128, 512], F32))
                    for i in range(2)]
            r_row = [bst.enter_context(nc.sbuf_tensor(f"rrow{s}_{i}", [1, 512], BF16))
                     for i in range(2)]
            lrow32 = bst.enter_context(nc.sbuf_tensor(f"lrow{s}", [1, 512], F32))
            ofm = bst.enter_context(nc.sbuf_tensor(f"ofm{s}", [128, 8 * 512], BF16))
            atsb = [bst.enter_context(nc.sbuf_tensor(f"atsb{s}_{i}", [128, 1024], F32))
                    for i in range(2)]
            xtB = [bst.enter_context(nc.sbuf_tensor(f"xtB{s}", [128, 1024], F32))]
            x2t = [bst.enter_context(nc.sbuf_tensor(f"x2t{s}_{i}", [128, 1024], F32))
                   for i in range(2)]
            ln3t = [bst.enter_context(nc.sbuf_tensor(f"ln3t{s}_{i}", [128, 1024], BF16))
                    for i in range(2)]
            stwB = [bst.enter_context(nc.sbuf_tensor(f"stB{s}_{i}", [128, 16], F32))
                    for i in range(2)]
            s_ps = [bst.enter_context(nc.psum_tensor(f"sps{s}_{i}", [128, 1024], F32))
                    for i in range(2)]
            ptps = [bst.enter_context(nc.psum_tensor(f"ptps{s}_0", [128, 1024], BF16))]
            o_ps = [bst.enter_context(nc.psum_tensor(f"ops{s}_{i}", [128, 512], F32))
                    for i in range(2)]
            at_ps = bst.enter_context(nc.psum_tensor(f"atps{s}", [128, 512], F32))
            rb_ps = at_ps

            em.op("gpsimd", "dma_start", wo_sb[:], dr["wo"][:], incs=[(T["dWo"], 16)])
            tok_wo = T["dWo"].v

            sps_rd = [0, 0]
            pt_rd = [0, 0]
            ptps_rd = [[]]
            ops_rd = [[], []]
            osb_rd = [0, 0]
            rrow_rd = [0, 0]
            rb_rd = 0
            atps_rd = 0
            xtB_tok = [0, 0]
            xtB_rd = [0, 0]
            x2t_rd = [[], []]
            stB_rd = [0, 0]
            ln3t_rd = [0, 0]
            atsb_rd = [0, 0]
            pending_ln3 = []
            mask_wait = True

            rr = 0
            orot = 0
            def emit_pair_mms(qg, c):
                nonlocal rr, mask_wait
                ch = c
                nkt = 4 * (qg + 1)
                exp_tok = [0] * nkt

                def emit_st(kt):
                    nonlocal rr, mask_wait
                    sl = rr % 2
                    rr += 1
                    em.wait("tensor", sS, sps_rd[sl])
                    for hh in range(2):
                        pb = 64 * hh
                        em.op("tensor", "matmul",
                              s_ps[sl][:, hh * 512:(hh + 1) * 512],
                              k_fm[pb:pb + 64, ch * 2048 + kt * 128: ch * 2048 + (kt + 1) * 128],
                              q_fm[pb:pb + 64, ch * 2048 + qg * 512: ch * 2048 + (qg + 1) * 512],
                              start=True, stop=True, incs=[(tPE, 1)])
                    tok_st = (tPE, tPE.v)
                    r = kt - qg * 4
                    if r >= 0:
                        em.wait("vector", tPE, tok_st[1])
                        if mask_wait:
                            em.wait("vector", dPro, T["tok_prologue"])
                            mask_wait = False
                        for hh in range(2):
                            em.op("vector", "tensor_tensor",
                                  s_ps[sl][:, hh * 512:(hh + 1) * 512],
                                  s_ps[sl][:, hh * 512:(hh + 1) * 512],
                                  T["masks"][:, r * 512:(r + 1) * 512],
                                  ALU.add, incs=[(vV, 1)])
                        tok_st = (vV, vV.v)
                    em.wait("scalar", tok_st[0], tok_st[1])
                    em.wait("scalar", tPE, pt_rd[sl])
                    em.op("scalar", "activation", pt_sb[sl][:], s_ps[sl][:],
                          AF.Exp, bias=EXPB, incs=[(sS, 1)])
                    sps_rd[sl] = sS.v
                    exp_tok[kt] = sS.v
                    return sl

                slots = [emit_st(kk) for kk in range(min(2, nkt))]
                for kt in range(nkt):
                    sl = slots[kt]
                    em.wait("tensor", sS, exp_tok[kt])
                    if kt == 0:
                        em.waits("tensor", ops_rd[0])
                        em.waits("tensor", ops_rd[1])
                        ops_rd[0] = []
                        ops_rd[1] = []
                    for hh in range(2):
                        h = 2 * c + hh
                        em.op("tensor", "matmul", o_ps[hh][0:65, :],
                              v_tm[:, kt * 1040 + h * 65: kt * 1040 + (h + 1) * 65],
                              pt_sb[sl][:, hh * 512:(hh + 1) * 512],
                              start=(kt == 0), stop=(kt == nkt - 1),
                              incs=([(tPE, 1)] if hh == 1 else []))
                    pt_rd[sl] = tPE.v
                    if kt + 2 < nkt:
                        slots.append(emit_st(kt + 2))
                return (c, tPE.v)

            def emit_fin_a(fin):
                c, tok_pv = fin
                toks = []
                for hh in range(2):
                    em.wait("scalar", tPE, tok_pv)
                    em.wait("scalar", vV, osb_rd[hh])
                    em.op("scalar", "activation", o_sb[hh][0:64, 0:512],
                          o_ps[hh][0:64, :], AF.Identity, incs=[(sS, 1)])
                    tok_osb = sS.v
                    em.op("scalar", "activation", lrow32[0:1, :],
                          o_ps[hh][64:65, 0:512], AF.Ln, incs=[(sS, 1)])
                    em.wait("scalar", sS)
                    em.wait("scalar", tPE, rrow_rd[hh])
                    em.op("scalar", "activation", r_row[hh][0:1, :],
                          lrow32[0:1, :], AF.Exp, scale=NEG1_S, incs=[(sS, 1)])
                    tok_rr = sS.v
                    ops_rd[hh] = [(sS, tok_rr)]
                    toks.append((tok_osb, tok_rr))
                return (c, tok_pv, toks)

            def emit_fin_b(fin):
                nonlocal rb_rd
                c, tok_pv, toks = fin
                for hh in range(2):
                    tok_osb, tok_rr = toks[hh]
                    pb = 64 * hh
                    em.wait("tensor", sS, tok_rr)
                    em.wait("tensor", vV, rb_rd)
                    em.wait("tensor", sS, atps_rd)
                    em.op("tensor", "matmul", rb_ps[0:64, :],
                          T["onec"][0:1, 0:64], r_row[hh][0:1, :],
                          start=True, stop=True, incs=[(tPE, 1)])
                    tok_rb = tPE.v
                    rrow_rd[hh] = tok_rb
                    em.wait("vector", tPE, tok_rb)
                    em.wait("vector", sS, tok_osb)
                    em.op("vector", "tensor_tensor",
                          ofm[pb:pb + 64, c * 512:(c + 1) * 512],
                          o_sb[hh][0:64, 0:512], rb_ps[0:64, :], ALU.mult,
                          incs=[(vV, 1)])
                    rb_rd = vV.v
                    osb_rd[hh] = vV.v

            pending_fin = None
            for qg in range(NQG):
                for c in range(H // 2):
                    if pending_fin is not None:
                        pending_fin = emit_fin_a(pending_fin)
                    fin = emit_pair_mms(qg, c)
                    if pending_fin is not None:
                        emit_fin_b(pending_fin)
                    pending_fin = fin
                pending_fin = emit_fin_a(pending_fin)
                emit_fin_b(pending_fin)
                pending_fin = None
                tok_ofm = vV.v
                def emit_ln3_tp(job):
                    qt_, slx_, tok_apply = job
                    em.wait("tensor", sS, tok_apply)
                    em.waits("tensor", ptps_rd[0])
                    ptps_rd[0] = []
                    for c_ in range(8):
                        em.op("tensor", "transpose",
                              ptps[0][:, c_ * 128:(c_ + 1) * 128],
                              ln3t[slx_][:, c_ * 128:(c_ + 1) * 128], ident[:],
                              incs=([(tPE, 1)] if c_ == 7 else []))
                    ln3t_rd[slx_] = tPE.v
                    em.wait("vector", tPE, tPE.v)
                    em.op("vector", "tensor_copy",
                          fmx[:].rearrange("p (c t) -> p c t", c=8)[:, :, qt_ * 128:(qt_ + 1) * 128],
                          ptps[0][:].rearrange("p (c u) -> p c u", c=8),
                          incs=[(vV, 1)])
                    ptps_rd[0].append((vV, vV.v))

                for jt in range(4):
                    qt = qg * 4 + jt
                    slx = jt % 2
                    em.wait("sync", vV, xtB_rd[0])
                    em.op("sync", "dma_start", xtB[0][:],
                          xin[s, qt * 128:(qt + 1) * 128, :], incs=[(dXb, 16)])
                    xtB_tok[0] = dXb.v
                    em.wait("tensor", vV, tok_ofm)
                    em.wait("tensor", T["dWo"], tok_wo)
                    em.wait("tensor", vV, rb_rd)
                    for fo in range(2):
                        em.wait("tensor", sS, atps_rd)
                        for kc in range(8):
                            em.op("tensor", "matmul",
                                  at_ps[:],
                                  ofm[:, kc * 512 + jt * 128: kc * 512 + (jt + 1) * 128],
                                  wo_sb[:, kc * 1024 + fo * 512: kc * 1024 + (fo + 1) * 512],
                                  start=(kc == 0), stop=(kc == 7),
                                  incs=([(tPE, 1)] if kc == 7 else []))
                        em.wait("scalar", tPE, tPE.v)
                        if fo == 0:
                            em.wait("scalar", vV, atsb_rd[slx])
                        em.op("scalar", "activation",
                              atsb[slx][:, fo * 512:(fo + 1) * 512], at_ps[:],
                              AF.Identity, incs=[(sS, 1)])
                        atps_rd = sS.v
                    tok_wmm = tPE.v
                    if len(pending_ln3) >= 2:
                        emit_ln3_tp(pending_ln3.pop(0))
                    tok_atsb = sS.v
                    em.wait("vector", sS, stB_rd[slx])
                    r_ap, nm_ap, tok_nm, tok_r = _ln_stats(
                        em, T, stwB[slx], atsb[slx][:], T["scr"], 0,
                        [(sS, tok_atsb)], [(sS, tok_atsb)])
                    em.wait("vector", dXb, xtB_tok[0])
                    em.wait("vector", vV, tok_nm)
                    em.op("vector", "tensor_scalar", T["scr"][:], xtB[0][:],
                          nm_ap, None, ALU.add, incs=[(vV, 1)])
                    xtB_rd[0] = vV.v
                    em.wait("vector", sS, tok_r)
                    em.waits("vector", x2t_rd[slx])
                    x2t_rd[slx] = []
                    em.wait("vector", vV)
                    em.op("vector", "scalar_tensor_tensor", x2t[slx][:],
                          atsb[slx][:], r_ap, T["scr"][:], ALU.mult, ALU.add,
                          incs=[(vV, 1)])
                    atsb_rd[slx] = vV.v
                    tok_x2 = vV.v
                    em.wait("sync", vV, tok_x2)
                    em.op("sync", "dma_start", x2scr[s, qt * 128:(qt + 1) * 128, :],
                          x2t[slx][:], incs=[(dS[slx], 16)])
                    x2t_rd[slx].append((dS[slx], dS[slx].v))
                    r3_ap, nm3_ap, tok_nm3, tok_r3 = _ln_stats(
                        em, T, stwB[slx], x2t[slx][:], T["scr"], 8,
                        [(vV, tok_x2)], [(vV, tok_x2)])
                    em.wait("scalar", vV, tok_nm3)
                    em.wait("scalar", tPE, ln3t_rd[slx])
                    em.op("scalar", "activation", ln3t[slx][:], x2t[slx][:],
                          AF.Identity, bias=nm3_ap, scale=r3_ap, incs=[(sS, 1)])
                    stB_rd[slx] = sS.v
                    x2t_rd[slx].append((sS, sS.v))
                    pending_ln3.append((qt, slx, sS.v))
            while pending_ln3:
                emit_ln3_tp(pending_ln3.pop(0))
            em.flush()

    if "F" in _PHASES:
        _emit_ffn(nc, st, em, s, T)


def _emit_ffn(nc, st, em, s, T):
    xout, x2scr, dr = T["xout"], T["x2scr"], T["dr"]
    fmx = T["fmx"]
    dPro = T["dPro"]
    dF13 = [T["dF13_0"], T["dF13_1"]]
    dF2 = [T["dF2_0"], T["dF2_1"], T["dF2_2"], T["dF2_3"]]
    dFc = T["dFc"]
    dS = [T["dS0"], T["dS1"]]
    dXf = [T["dXf0"], T["dXf1"]]
    dO = [T["dO0"], T["dO1"]]
    tPE, vV, sS = T["tPE"], T["vV"], T["sS"]
    fm = lambda c, t0, n: fmx[:, c * 2048 + t0: c * 2048 + t0 + n]

    for _e in ("sync", "gpsimd", "tensor", "vector", "scalar"):
        em.wait(_e, T["dS0"])
        em.wait(_e, T["dS1"])
        em.wait(_e, T["dXb"])
    with ExitStack() as fst:
        wbuf = [fst.enter_context(nc.sbuf_tensor(f"wF{s}_{i}", [128, 8192], BF16))
                for i in range(2)]
        w2buf = [fst.enter_context(nc.sbuf_tensor(f"w2F{s}_{i}", [128, 4096], BF16))
                 for i in range(4)]
        hsb = [fst.enter_context(nc.sbuf_tensor(f"hs{s}_{i}", [128, 4096], BF16))
               for i in range(4)]
        ysb = [fst.enter_context(nc.sbuf_tensor(f"ysb{s}_{j}", [128, 1024], F32))
               for j in range(JT_G)]
        h1sb = fst.enter_context(nc.sbuf_tensor(f"h1sb{s}", [128, 512], F32))
        hstb = fst.enter_context(nc.sbuf_tensor(f"hstF{s}", [128, 512], F32))
        gwork = fst.enter_context(nc.sbuf_tensor(f"gwork{s}", [128, 1024], F32))
        comb = fst.enter_context(nc.sbuf_tensor(f"comb{s}", [128, 128], BF16))
        cemb = [fst.enter_context(nc.sbuf_tensor(f"cemb{s}_{i}", [16, FG], BF16))
                for i in range(2)]
        x2tF = [fst.enter_context(nc.sbuf_tensor(f"x2F{s}_{i}", [128, 1024], F32))
                for i in range(2)]
        outt = [fst.enter_context(nc.sbuf_tensor(f"outt{s}_{i}", [128, 1024], F32))
                for i in range(2)]
        stwF = fst.enter_context(nc.sbuf_tensor(f"stF{s}", [128, 16], F32))
        hps = [fst.enter_context(nc.psum_tensor(f"hps{s}_{i}", [128, 512], F32))
               for i in range(2)]
        cbps = fst.enter_context(nc.psum_tensor(f"cbps{s}", [128, 512], F32))
        yps = [fst.enter_context(nc.psum_tensor(f"yps{s}_{i}", [128, 1024], F32))
               for i in range(2)]
        ctps = fst.enter_context(nc.psum_tensor(f"ctps{s}", [128, 1024], BF16))
        selm = fst.enter_context(nc.sbuf_tensor(f"selmF{s}", [16, 2048], BF16))
        be2s = fst.enter_context(nc.sbuf_tensor(f"be2F{s}", [16, 1024], BF16))
        bs2s = fst.enter_context(nc.sbuf_tensor(f"bs2F{s}", [1, 1024], BF16))
        em.op("gpsimd", "dma_start", selm[:], dr["csel"][:], incs=[(dFc, 16)])
        em.op("gpsimd", "dma_start", be2s[:], dr["be2"][:], incs=[(dFc, 16)])
        em.op("gpsimd", "dma_start", bs2s[:], dr["bs2"][:], incs=[(dFc, 16)])
        tok_fconst = dFc.v

        wbuf_rd = [0, 0]
        w2_rd = [0] * 4
        hs_rd = [0] * 4
        hs_tok = [0] * NE
        hps_rd = [[], []]
        h1sb_rd = [0]
        cb_rd = 0
        yps_rd = [0, 0]
        ysb_rd = [0] * JT_G
        cemb_rd = [0, 0]
        comb_rd = 0
        ctps_rd = 0
        x2F_rd = [0, 0]
        out_rd = [0, 0]
        x2_stores = (dS[0].v, dS[1].v)

        def emit_gate(g):
            nonlocal cb_rd, comb_rd, ctps_rd
            GW = gwork
            for jt in range(JT_G):
                qt = g * JT_G + jt
                em.wait("tensor", vV, cb_rd)
                if s == 0 and g == 0 and jt == 0:
                    em.wait("tensor", dPro, T["tok_prologue"])
                for kc in range(8):
                    em.op("tensor", "matmul", cbps[:, jt * 16:(jt + 1) * 16],
                          fm(kc, qt * 128, 128), T["gwt"][:, kc * 16:(kc + 1) * 16],
                          start=(kc == 0), stop=(kc == 7),
                          incs=([(tPE, 1)] if kc == 7 else []))
            tok_gmm = tPE.v
            em.wait("vector", tPE, tok_gmm)
            em.op("vector", "tensor_copy", GW[:, 0:128], cbps[:, 0:128],
                  incs=[(vV, 1)])
            cb_rd = vV.v
            em.wait("scalar", vV, vV.v)
            em.op("scalar", "activation", GW[:, 0:128], GW[:, 0:128], AF.Exp,
                  incs=[(sS, 1)])
            gv = lambda off: GW[:, off:off + 128].rearrange("p (j i) -> p j i", i=16)
            bc8 = lambda off: bass.AP(gwork, off, [[1024, 128], [1, 8], [0, 16]])
            bc8g = lambda off: bass.AP(gwork, off, [[1024, 128], [1, 8], [0, 4]])
            em.wait("vector", sS, sS.v)
            em.op("vector", "tensor_reduce", GW[:, 608:616],
                  gv(0), AX.X, ALU.add, incs=[(vV, 1)])
            em.wait("vector", vV)
            em.op("vector", "reciprocal", GW[:, 616:624], GW[:, 608:616],
                  incs=[(vV, 1)])
            em.wait("vector", vV)
            em.op("vector", "tensor_tensor", gv(128), gv(0), bc8(616),
                  ALU.mult, incs=[(vV, 1)])
            em.wait("vector", vV)
            em.op("vector", "tensor_tensor", gv(256), gv(128),
                  bass.AP(T["gb16"], 0, [[16, 128], [0, 8], [1, 16]]),
                  ALU.add, incs=[(vV, 1)])
            em.wait("vector", vV)
            pv = lambda i: bass.AP(gwork, 256 + i, [[1024, 128], [16, 8], [4, 4]])
            pi = 0
            for i in range(4):
                for j in range(i + 1, 4):
                    em.op("vector", "tensor_tensor",
                          GW[:, 384 + pi * 32:384 + pi * 32 + 32].rearrange(
                              "p (j i) -> p j i", i=4),
                          pv(i), pv(j), ALU.add, incs=[(vV, 1)])
                    pi += 1
            em.wait("vector", vV)
            em.op("vector", "tensor_tensor", GW[:, 576:608], GW[:, 384:416],
                  GW[:, 416:448], ALU.max, incs=[(vV, 1)])
            for pi in range(2, 6):
                em.wait("vector", vV)
                em.op("vector", "tensor_tensor", GW[:, 576:608], GW[:, 576:608],
                      GW[:, 384 + pi * 32:416 + pi * 32], ALU.max, incs=[(vV, 1)])
            gsv = GW[:, 576:608].rearrange("p (j i) -> p j i", i=4)
            em.wait("vector", vV)
            em.op("vector", "tensor_reduce", GW[:, 624:632], gsv, AX.X, ALU.max,
                  incs=[(vV, 1)])
            em.wait("vector", vV)
            em.op("vector", "tensor_tensor", GW[:, 640:672].rearrange(
                      "p (j i) -> p j i", i=4), gsv, bc8g(624),
                  ALU.is_ge, incs=[(vV, 1)])
            em.wait("vector", vV)
            em.op("vector", "scalar_tensor_tensor", GW[:, 672:704],
                  GW[:, 640:672], NBIG, GW[:, 576:608], ALU.mult, ALU.add,
                  incs=[(vV, 1)])
            em.wait("vector", vV)
            em.op("vector", "tensor_reduce", GW[:, 632:640],
                  GW[:, 672:704].rearrange("p (j i) -> p j i", i=4), AX.X,
                  ALU.max, incs=[(vV, 1)])
            em.wait("vector", vV)
            em.op("vector", "tensor_tensor", GW[:, 640:672].rearrange(
                      "p (j i) -> p j i", i=4), gsv, bc8g(632),
                  ALU.is_ge, incs=[(vV, 1)])
            em.wait("vector", vV)
            em.op("vector", "tensor_copy",
                  GW[:, 704:832].rearrange("p (j g i) -> p j g i", g=4, i=4),
                  bass.AP(gwork, 640, [[1024, 128], [4, 8], [1, 4], [0, 4]]),
                  incs=[(vV, 1)])
            em.wait("vector", vV)
            em.op("vector", "tensor_scalar", GW[:, 704:832], GW[:, 704:832],
                  1.0, BIG, ALU.subtract, ALU.mult, incs=[(vV, 1)])
            em.wait("vector", vV)
            em.op("vector", "tensor_tensor", GW[:, 384:512], GW[:, 256:384],
                  GW[:, 704:832], ALU.add, incs=[(vV, 1)])
            em.wait("vector", vV)
            em.op("vector", "tensor_reduce", GW[:, 624:632], gv(384), AX.X,
                  ALU.max, incs=[(vV, 1)])
            em.wait("vector", vV)
            em.op("vector", "tensor_tensor", gv(512), gv(384), bc8(624),
                  ALU.is_ge, incs=[(vV, 1)])
            em.wait("vector", vV)
            em.op("vector", "scalar_tensor_tensor", GW[:, 640:768],
                  GW[:, 512:640], NBIG, GW[:, 384:512], ALU.mult, ALU.add,
                  incs=[(vV, 1)])
            em.wait("vector", vV)
            em.op("vector", "tensor_reduce", GW[:, 632:640], gv(640), AX.X,
                  ALU.max, incs=[(vV, 1)])
            em.wait("vector", vV)
            em.op("vector", "tensor_tensor", gv(512), gv(384), bc8(632),
                  ALU.is_ge, incs=[(vV, 1)])
            em.wait("vector", vV)
            em.wait("vector", tPE, comb_rd)
            em.op("vector", "tensor_tensor", comb[:], GW[:, 128:256],
                  GW[:, 512:640], ALU.mult, incs=[(vV, 1)])
            tok_comb = vV.v
            for jt in range(JT_G):
                em.wait("tensor", vV, tok_comb)
                em.wait("tensor", vV, ctps_rd)
                em.op("tensor", "transpose", ctps[0:16, jt * 128:(jt + 1) * 128],
                      comb[:, jt * 16:(jt + 1) * 16], T["ident"][:],
                      incs=[(tPE, 1)])
            comb_rd = tPE.v
            em.wait("vector", tPE, tPE.v)
            em.wait("vector", tPE, cemb_rd[g])
            em.op("vector", "tensor_copy", cemb[g][:],
                  ctps[0:16, 0:1024], incs=[(vV, 1)])
            ctps_rd = vV.v
            cemb_tok[g] = vV.v

        cemb_tok = [0, 0]
        emit_gate(0)
        emit_gate(1)

        for g in range(NFG):


            for ci, chunk in enumerate(CHUNKS):
                for e in chunk:
                    wsl = e % 2
                    w2sl = (e % 4) if e < 16 else (e - 16)
                    em.wait("gpsimd", tPE, wbuf_rd[wsl])
                    em.op("gpsimd", "dma_start", wbuf[wsl][:], dr["we13"][e, :, :],
                          incs=[(dF13[wsl], 16)])
                    tok_w13 = (dF13[wsl], dF13[wsl].v)
                    em.wait("gpsimd", tPE, w2_rd[w2sl])
                    em.op("gpsimd", "dma_start", w2buf[w2sl][:], dr["we2"][e, :, :],
                          incs=[(dF2[w2sl], 16)])
                    for tch in range(FG // 512):
                        if e < 16:
                            em.wait("tensor", vV, cb_rd)
                            em.wait("tensor", vV, cemb_tok[g])
                            em.wait("tensor", dFc, tok_fconst)
                            em.op("tensor", "matmul", cbps[:],
                                  selm[:, e * 128:(e + 1) * 128],
                                  cemb[g][:, tch * 512:(tch + 1) * 512],
                                  start=True, stop=True, incs=[(tPE, 1)])
                            tok_cb = tPE.v
                        for ic in range(4):
                            s1, s3 = 0, 1
                            em.wait("tensor", tok_w13[0], tok_w13[1])
                            em.waits("tensor", hps_rd[s1])
                            hps_rd[s1] = []
                            for kc in range(8):
                                em.op("tensor", "matmul", hps[s1][:],
                                      wbuf[wsl][:, kc * 1024 + ic * 128: kc * 1024 + ic * 128 + 128],
                                      fm(kc, g * FG + tch * 512, 512),
                                      start=(kc == 0), stop=(kc == 7),
                                      incs=([(tPE, 1)] if kc == 7 else []))
                            tok_h1 = tPE.v
                            em.waits("tensor", hps_rd[s3])
                            hps_rd[s3] = []
                            for kc in range(8):
                                em.op("tensor", "matmul", hps[s3][:],
                                      wbuf[wsl][:, kc * 1024 + 512 + ic * 128: kc * 1024 + 512 + ic * 128 + 128],
                                      fm(kc, g * FG + tch * 512, 512),
                                      start=(kc == 0), stop=(kc == 7),
                                      incs=([(tPE, 1)] if kc == 7 else []))
                            tok_h3 = tPE.v
                            wbuf_rd[wsl] = tok_h3
                            em.wait("scalar", tPE, tok_h1)
                            em.wait("scalar", vV, h1sb_rd[0])
                            em.op("scalar", "activation", h1sb[:], hps[s1][:],
                                  AF.Silu,
                                  bias=T["bia1"][:, e * 4 + ic:e * 4 + ic + 1],
                                  incs=[(sS, 1)])
                            hps_rd[s1].append((sS, sS.v))
                            em.wait("vector", tPE, tok_h3)
                            em.wait("vector", sS, sS.v)
                            em.wait("vector", vV)
                            em.op("vector", "scalar_tensor_tensor", hstb[:],
                                  hps[s3][:],
                                  T["bia3"][:, e * 4 + ic:e * 4 + ic + 1],
                                  h1sb[:], ALU.add, ALU.mult, incs=[(vV, 1)])
                            hps_rd[s3].append((vV, vV.v))
                            h1sb_rd[0] = vV.v
                            em.wait("vector", vV)
                            em.wait("vector", tPE, hs_rd[w2sl])
                            dsth = hsb[w2sl][:, ic * 1024 + tch * 512: ic * 1024 + (tch + 1) * 512]
                            if e < 16:
                                em.wait("vector", tPE, tok_cb)
                                em.op("vector", "tensor_tensor", dsth, hstb[:],
                                      cbps[:], ALU.mult, incs=[(vV, 1)])
                                cb_rd = vV.v
                            else:
                                em.op("vector", "tensor_copy", dsth, hstb[:],
                                      incs=[(vV, 1)])
                    hs_tok[e] = vV.v
                tok_w2_all = [(d, d.v) for d in dF2]
                nch = len(chunk)
                for jt in range(JT_G):
                    ysl = jt % 2
                    em.waits("tensor", tok_w2_all)
                    em.wait("tensor", vV, yps_rd[ysl])
                    if ci == 0:
                        em.wait("tensor", vV, cemb_tok[g])
                        em.wait("tensor", dFc, tok_fconst)
                        for fo in range(2):
                            em.op("tensor", "matmul",
                                  yps[ysl][:, fo * 512:(fo + 1) * 512],
                                  cemb[g][:, jt * 128:(jt + 1) * 128],
                                  be2s[:, fo * 512:(fo + 1) * 512],
                                  start=True, stop=False)
                            em.op("tensor", "matmul",
                                  yps[ysl][:, fo * 512:(fo + 1) * 512],
                                  T["onec"][:],
                                  bs2s[:, fo * 512:(fo + 1) * 512],
                                  start=False, stop=False)
                    for i4, e in enumerate(chunk):
                        w2sl = (e % 4) if e < 16 else (e - 16)
                        em.wait("tensor", vV, hs_tok[e])
                        for ic in range(4):
                            st_flag = (ci != 0 and i4 == 0 and ic == 0)
                            for fo in range(2):
                                em.op("tensor", "matmul",
                                      yps[ysl][:, fo * 512:(fo + 1) * 512],
                                      hsb[w2sl][:, ic * 1024 + jt * 128: ic * 1024 + jt * 128 + 128],
                                      w2buf[w2sl][:, ic * 1024 + fo * 512: ic * 1024 + (fo + 1) * 512],
                                      start=st_flag,
                                      stop=(i4 == nch - 1 and ic == 3),
                                      incs=([(tPE, 1)] if (i4 == nch - 1 and ic == 3 and fo == 1) else []))
                    tok_y = tPE.v
                    cemb_rd[g] = tok_y
                    for e in chunk:
                        w2sl = (e % 4) if e < 16 else (e - 16)
                        w2_rd[w2sl] = tok_y
                        hs_rd[w2sl] = tok_y
                    em.wait("vector", tPE, tok_y)
                    em.wait("vector", sS, ysb_rd[jt])
                    if ci == 0:
                        em.op("vector", "tensor_copy", ysb[jt][:], yps[ysl][:],
                              incs=[(vV, 1)])
                    else:
                        em.op("vector", "tensor_tensor", ysb[jt][:], ysb[jt][:],
                              yps[ysl][:], ALU.add, incs=[(vV, 1)])
                    yps_rd[ysl] = vV.v

            for jt in range(JT_G):
                qt = g * JT_G + jt
                slx = jt % 2
                em.wait("sync", T["dS0"], x2_stores[0])
                em.wait("sync", T["dS1"], x2_stores[1])
                em.wait("sync", vV, x2F_rd[slx])
                em.op("sync", "dma_start", x2tF[slx][:],
                      x2scr[s, qt * 128:(qt + 1) * 128, :], incs=[(dXf[slx], 16)])
                tok_x2l = dXf[slx].v
                r_ap, nm_ap, tok_nm, tok_r = _ln_stats(
                    em, T, stwF, ysb[jt][:], T["scr"], 0,
                    [(vV, vV.v)], [(vV, vV.v)])
                ysb_rd[jt] = sS.v
                em.wait("vector", dXf[slx], tok_x2l)
                em.wait("vector", vV, tok_nm)
                em.op("vector", "tensor_scalar", T["scr"][:], x2tF[slx][:],
                      nm_ap, None, ALU.add, incs=[(vV, 1)])
                x2F_rd[slx] = vV.v
                em.wait("vector", sS, tok_r)
                em.wait("vector", dO[slx], out_rd[slx])
                em.wait("vector", vV)
                em.op("vector", "scalar_tensor_tensor", outt[slx][:], ysb[jt][:],
                      r_ap, T["scr"][:], ALU.mult, ALU.add, incs=[(vV, 1)])
                tok_out = vV.v
                em.wait("sync", vV, tok_out)
                em.op("sync", "dma_start", xout[s, qt * 128:(qt + 1) * 128, :],
                      outt[slx][:], incs=[(dO[slx], 16)])
                out_rd[slx] = dO[slx].v
        em.flush()


_NC_CACHE = None


def _get_nc():
    global _NC_CACHE
    if _NC_CACHE is None:
        _NC_CACHE = build_program()
    return _NC_CACHE


def _layout_weights(inputs):
    f = np.float32
    out = {}
    chunk_lhs = lambda w: np.ascontiguousarray(
        w.reshape(8, 128, -1).transpose(1, 0, 2).reshape(128, -1))
    Wkv = np.asarray(inputs["L0_attn_Wkv"], f)
    out["wq"] = chunk_lhs(np.asarray(inputs["L0_attn_Wq"], f))
    out["wk"] = chunk_lhs(np.ascontiguousarray(Wkv[:, :D]))
    out["wv"] = chunk_lhs(np.ascontiguousarray(Wkv[:, D:]))
    out["wo"] = chunk_lhs(np.asarray(inputs["L0_attn_Wo"], f))
    gw = np.asarray(inputs["L0_ffn_gate_w"], f)
    out["gwt"] = chunk_lhs(np.ascontiguousarray(gw.T))
    out["gb16"] = np.ascontiguousarray(
        np.broadcast_to(np.asarray(inputs["L0_ffn_gate_b"], f)[None, :], (128, 16)))
    We1 = np.asarray(inputs["L0_ffn_We1"], f)
    We3 = np.asarray(inputs["L0_ffn_We3"], f)
    We2 = np.asarray(inputs["L0_ffn_We2"], f)
    ws1 = np.asarray(inputs["L0_ffn_ws1"], f)
    ws3 = np.asarray(inputs["L0_ffn_ws3"], f)
    ws2 = np.asarray(inputs["L0_ffn_ws2"], f)
    w13 = np.empty((NE, 128, 8192), f)
    w2 = np.empty((NE, 128, 4096), f)
    for e in range(NE):
        if e < 16:
            a, b, c = We1[e], We3[e], We2[e]
        else:
            hf = e - 16
            a = ws1[:, hf * 512:(hf + 1) * 512]
            b = ws3[:, hf * 512:(hf + 1) * 512]
            c = ws2[hf * 512:(hf + 1) * 512, :]
        ar = np.ascontiguousarray(a).reshape(8, 128, 512)
        br = np.ascontiguousarray(b).reshape(8, 128, 512)
        w13[e] = np.concatenate([ar, br], axis=2).transpose(1, 0, 2).reshape(128, 8192)
        w2[e] = np.ascontiguousarray(c).reshape(4, 128, 1024).transpose(1, 0, 2).reshape(128, 4096)
    out["we13"] = w13
    out["we2"] = w2
    lay_b = lambda be, bsh: np.concatenate(
        [be.reshape(16, 4, 128).transpose(2, 0, 1).reshape(128, 64),
         bsh.reshape(2, 4, 128).transpose(2, 0, 1).reshape(128, 8)], axis=1)
    out["bia1"] = np.ascontiguousarray(
        lay_b(np.asarray(inputs["L0_ffn_be1"], f), np.asarray(inputs["L0_ffn_bs1"], f)))
    out["bia3"] = np.ascontiguousarray(
        lay_b(np.asarray(inputs["L0_ffn_be3"], f), np.asarray(inputs["L0_ffn_bs3"], f)))
    out["be2"] = np.asarray(inputs["L0_ffn_be2"], f)
    out["bs2"] = np.asarray(inputs["L0_ffn_bs2"], f).reshape(1, 1024)
    out["cident"] = np.eye(128, dtype=f)
    kl = np.arange(128)[:, None]
    ql = np.arange(512)[None, :]
    out["cmask"] = np.concatenate(
        [np.where(ql >= r * 128 + kl, 0.0, -50000.0).astype(f) for r in range(4)],
        axis=1)
    sel = np.zeros((16, 2048), f)
    for e in range(16):
        sel[e, e * 128:(e + 1) * 128] = 1.0
    out["csel"] = sel
    return out


def kernel(**inputs):
    nc = _get_nc()
    w = _layout_weights(inputs)
    x = np.asarray(inputs["L0_attn_x"], np.float32)
    in_maps = []
    for c in range(8):
        m = dict(w)
        m["xin"] = np.ascontiguousarray(x[c * NSEQ:(c + 1) * NSEQ])
        in_maps.append(m)
    res = run_bass_kernel_spmd(nc, in_maps, core_ids=list(range(8)))
    return np.concatenate([np.asarray(r["xout"], np.float32) for r in res.results],
                          axis=0)


if __name__ == "__main__":
    import time
    t0 = time.time()
    nc = build_program()
    n = sum(1 for _ in nc.all_instructions())
    print(f"built in {time.time()-t0:.1f}s, {n} instructions")
